# version 02
# baseline (speedup 1.0000x reference)
import contextlib
import numpy as np
import ml_dtypes
import concourse.bass as bass
import concourse.mybir as mybir
from concourse.bass_utils import run_bass_kernel_spmd

F32 = mybir.dt.float32
BF16 = mybir.dt.bfloat16
I32 = mybir.dt.int32
AF = mybir.ActivationFunctionType
OP = mybir.AluOpType
AX = mybir.AxisListType
BF = ml_dtypes.bfloat16

BIG_NEG = -30000.0
D = 1024
NFC = 8
H = 16
DH = 64
S = 8192
NT = 64
VW = 16 * 65
EPS = 1e-5


def ts(i, n=128):
    return slice(i * n, (i + 1) * n)


def cs(c, T, j0=None, j1=None):
    if j0 is None:
        return slice(c * T, (c + 1) * T)
    return slice(c * T + j0, c * T + (j1 if j1 is not None else j0))


class Sem:
    def __init__(self, nc, st, name):
        self.h = st.enter_context(nc.semaphore(name))
        self.n = 0

    def inc(self, instr, k=1):
        instr.then_inc(self.h, k)
        self.n += k
        return self.n

    def dma(self, instr):
        return self.inc(instr, 16)

    def wait(self, eng, val=None):
        eng.wait_ge(self.h, self.n if val is None else val)

    def chain(self, eng, instr, k=1):
        v = self.inc(instr, k)
        eng.wait_ge(self.h, v)
        return v


class Ctx:
    def __init__(self, nc, block, st):
        self.nc = nc
        self.block = block
        self.st = st
        self.ld = Sem(nc, st, "s_ld")
        self.wl = Sem(nc, st, "s_wl")
        self.mm = Sem(nc, st, "s_mm")
        self.mv = Sem(nc, st, "s_mv")
        self.ex = Sem(nc, st, "s_ex")
        self.vv = Sem(nc, st, "s_vv")
        self.so = Sem(nc, st, "s_so")
        self.sq = Sem(nc, st, "s_sq")
        self.mk = Sem(nc, st, "s_mk")
        self.ev = Sem(nc, st, "s_ev")
        self.wlA = Sem(nc, st, "s_wlA")
        self.wlB = Sem(nc, st, "s_wlB")
        self.PS = st.enter_context(nc.psum_tensor("PS", [128, 4096], F32))

    def sb(self, name, shape, dtype):
        return self.st.enter_context(self.nc.sbuf_tensor(name, shape, dtype))

    def bank(self, b, p0=0, p1=128, w=512):
        return self.PS[p0:p1, b * 512: b * 512 + w]

    def barrier(self):
        sems = [self.mm, self.mv, self.ex, self.vv, self.mk, self.ev,
                self.ld, self.so, self.sq]
        vals = [s.n for s in sems]
        def mk(eng):
            for s_, v_ in zip(sems, vals):
                if v_ > 0:
                    eng.wait_ge(s_.h, v_)
        self.phase(sync=mk, tensor=mk, scalar=mk, vector=mk, gpsimd=mk)

    def phase(self, **fns):
        for name in ("sync", "tensor", "scalar", "vector", "gpsimd"):
            if name in fns and fns[name] is not None:
                getattr(self.block, name)(fns[name])



def emit_setup(ctx, cn):
    def v(eng):
        eng.memset(cn["ones_col_b"][:, :], 1.0)
        eng.memset(cn["ones_row_b"][:, :], 1.0)
        eng.memset(cn["ones_row_f"][:, :], 1.0)
        eng.memset(cn["eps_col"][:, :], EPS)
        ctx.vv.inc(eng.drain())
    ctx.phase(vector=v)



def emit_ln(ctx, xf, xb, xn, T, sc, ready_ld=None):
    PSB = ctx.bank
    xsq = sc["xsq"]

    def v1(v):
        if ready_ld is not None:
            ready_ld[0].wait(v, ready_ld[1])
        for c in range(NFC):
            v.tensor_copy(out=xb[:, cs(c, T)], in_=xf[:, cs(c, T)])
        ctx.vv.inc(v.drain())
    ctx.phase(vector=v1)
    xb_ready = ctx.vv.n

    def s1(s):
        ctx.vv.wait(s, xb_ready)
        for c in range(NFC):
            s.activation(out=xsq[:, cs(c, T)], in_=xb[:, cs(c, T)], func=AF.Square)
        ctx.ex.inc(s.drain())
    ctx.phase(scalar=s1)
    sq_ready = ctx.ex.n

    def t1(t):
        ctx.vv.wait(t, xb_ready)
        for c in range(NFC):
            for j in range(0, T, 512):
                w = min(512, T - j)
                t.matmul(ctx.PS[0:1, j:j + w], sc["ones_col_b"][:, :],
                         xb[:, c * T + j: c * T + j + w],
                         start=(c == 0), stop=(c == NFC - 1))
        ctx.ex.wait(t, sq_ready)
        for c in range(NFC):
            for j in range(0, T, 512):
                w = min(512, T - j)
                t.matmul(ctx.PS[0:1, 1024 + j:1024 + j + w], sc["ones_col_b"][:, :],
                         xsq[:, c * T + j: c * T + j + w],
                         start=(c == 0), stop=(c == NFC - 1))
        ctx.mm.inc(t.drain())
    ctx.phase(tensor=t1)
    sums_ready = ctx.mm.n

    def v2(v):
        ctx.mm.wait(v, sums_ready)
        i1 = v.tensor_scalar(out=sc["rowm"][:, :], in0=ctx.PS[0:1, 0:T], scalar1=1.0 / D,
                             scalar2=None, op0=OP.mult)
        ctx.vv.chain(v, i1)
        i2 = v.tensor_scalar(out=sc["rowv"][:, :], in0=ctx.PS[0:1, 1024:1024 + T], scalar1=1.0 / D,
                             scalar2=None, op0=OP.mult)
        ctx.vv.chain(v, i2)
        i3 = v.tensor_mul(out=sc["rowa"][:, :], in0=sc["rowm"][:, :], in1=sc["rowm"][:, :])
        ctx.vv.chain(v, i3)
        i4 = v.tensor_sub(out=sc["rowv"][:, :], in0=sc["rowv"][:, :], in1=sc["rowa"][:, :])
        ctx.vv.inc(i4)
    ctx.phase(vector=v2)
    var_ready = ctx.vv.n

    def s2(s):
        ctx.vv.wait(s, var_ready)
        s.activation(out=sc["rowa"][:, :], in_=sc["rowv"][:, :], func=AF.Sqrt, bias=sc["eps_col"][0:1, 0:1])
        ctx.ex.inc(s.drain())
    ctx.phase(scalar=s2)
    sqrt_ready = ctx.ex.n

    def v3(v):
        ctx.ex.wait(v, sqrt_ready)
        i1 = v.reciprocal(out=sc["rowa"][:, :], in_=sc["rowa"][:, :])
        ctx.vv.chain(v, i1)
        i2 = v.scalar_tensor_tensor(out=sc["rowb"][:, :], in0=sc["rowm"][:, :], scalar=-1.0,
                                    in1=sc["rowa"][:, :], op0=OP.mult, op1=OP.mult)
        ctx.vv.inc(i2)
    ctx.phase(vector=v3)
    ab_ready = ctx.vv.n

    def t2(t):
        ctx.vv.wait(t, ab_ready)
        for j in range(0, T, 512):
            w = min(512, T - j)
            t.matmul(ctx.PS[:, j:j + w], sc["ones_row_f"][:, :],
                     sc["rowa"][0:1, j:j + w], start=True, stop=True)
        for j in range(0, T, 512):
            w = min(512, T - j)
            t.matmul(ctx.PS[:, 1024 + j:1024 + j + w], sc["ones_row_f"][:, :],
                     sc["rowb"][0:1, j:j + w], start=True, stop=True)
        ctx.mm.inc(t.drain())
    ctx.phase(tensor=t2)
    bc_ready = ctx.mm.n

    def v4(v):
        ctx.mm.wait(v, bc_ready)
        last = None
        for c in range(NFC):
            i1 = v.tensor_mul(out=xb[:, cs(c, T)], in0=xf[:, cs(c, T)], in1=ctx.PS[:, 0:T])
            ctx.vv.chain(v, i1)
            last = v.tensor_add(out=xn[:, cs(c, T)], in0=xb[:, cs(c, T)], in1=ctx.PS[:, 1024:1024 + T])
        ctx.vv.inc(last)
    ctx.phase(vector=v4)
    return ctx.vv.n



def emit_proj_fm(ctx, xn, w_sb, T, oc0, noc, out_sb, out_T, out_c0,
                 w_oc_total, bias_col=None, w_ready=None, xn_ready=None,
                 act=None, act_scale=1.0, bank0=4):
    assert T <= 1024
    nic = NFC
    ex_base = ctx.ex.n

    def t(eng):
        if xn_ready is not None:
            ctx.vv.wait(eng, xn_ready)
        if w_ready is not None:
            ctx.wl.wait(eng, w_ready)
        for i in range(noc):
            if i > 0:
                eng.wait_ge(ctx.ex.h, ex_base + i)
            for ic in range(nic):
                for j in range(0, T, 512):
                    w = min(512, T - j)
                    eng.matmul(ctx.PS[:, bank0 * 512 + j: bank0 * 512 + j + w],
                               w_sb[:, ic * w_oc_total + oc0 + i * 128: ic * w_oc_total + oc0 + (i + 1) * 128],
                               xn[:, ic * T + j: ic * T + j + w],
                               start=(ic == 0), stop=(ic == nic - 1))
            ctx.mm.inc(eng.drain())
    mm_base = ctx.mm.n
    ctx.phase(tensor=t)

    if act is not None:
        def s(eng):
            for i in range(noc):
                ctx.mm.wait(eng, mm_base + i + 1)
                kw = {}
                if bias_col is not None:
                    kw["bias"] = bias_col[:, i:i + 1]
                if isinstance(act_scale, tuple):
                    kw["scale"] = act_scale[0][:, act_scale[1] + i: act_scale[1] + i + 1]
                else:
                    kw["scale"] = float(act_scale)
                eng.activation(out=out_sb[:, (out_c0 + i) * out_T: (out_c0 + i) * out_T + T],
                               in_=ctx.PS[:, bank0 * 512: bank0 * 512 + T],
                               func=act, **kw)
                ctx.ex.inc(eng.drain())
        ctx.phase(scalar=s)
        return (ctx.ex, ctx.ex.n)
    else:
        def s(eng):
            for i in range(noc):
                ctx.mm.wait(eng, mm_base + i + 1)
                eng.tensor_scalar(
                    out=out_sb[:, (out_c0 + i) * out_T: (out_c0 + i) * out_T + T],
                    in0=ctx.PS[:, bank0 * 512: bank0 * 512 + T],
                    scalar1=(bias_col[:, i:i + 1] if bias_col is not None else 0.0),
                    scalar2=None, op0=OP.add)
                ctx.ex.inc(eng.drain())
        ctx.phase(vector=s)
        return (ctx.ex, ctx.ex.n)


def emit_proj_tok(ctx, xn, w_sb, T, out_sb, w_oc_total, vbias_bc=None,
                  w_ready=None, xn_ready=None, bank0=4, head65=True, xn_ap=None,
                  oc0=0):
    ntile = T // 128
    ow = VW if head65 else 1024
    vv_base = ctx.vv.n

    def t(eng):
        if xn_ready is not None:
            ctx.vv.wait(eng, xn_ready)
        if w_ready is not None:
            ctx.wl.wait(eng, w_ready)
        for i in range(ntile):
            if i > 0:
                eng.wait_ge(ctx.vv.h, vv_base + i)
            for ic in range(NFC):
                lhs = xn[:, ic * T + i * 128: ic * T + (i + 1) * 128] if xn_ap is None \
                    else xn_ap(ic, i)
                for j in range(2):
                    eng.matmul(ctx.PS[:, bank0 * 512 + j * 512: bank0 * 512 + (j + 1) * 512],
                               lhs, w_sb[:, ic * w_oc_total + oc0 + j * 512: ic * w_oc_total + oc0 + (j + 1) * 512],
                               start=(ic == 0), stop=(ic == NFC - 1))
            ctx.mm.inc(eng.drain())
    mm_base = ctx.mm.n
    ctx.phase(tensor=t)

    def v(eng):
        for i in range(ntile):
            ctx.mm.wait(eng, mm_base + i + 1)
            src = ctx.PS[:, bank0 * 512: bank0 * 512 + 1024]
            if head65:
                dst = out_sb[:, i * ow: (i + 1) * ow].rearrange("p (h w) -> p h w", h=16)[:, :, 0:64]
                srcv = src.rearrange("p (h w) -> p h w", h=16)
                if vbias_bc is not None:
                    eng.tensor_add(out=dst, in0=srcv, in1=vbias_bc[:, :].rearrange("p (h w) -> p h w", h=16))
                else:
                    eng.tensor_copy(out=dst, in_=srcv)
                eng.memset(out_sb[:, i * ow: (i + 1) * ow].rearrange("p (h w) -> p h w", h=16)[:, :, 64:65], 1.0)
            else:
                if vbias_bc is not None:
                    eng.tensor_add(out=out_sb[:, ts(i, 1024)], in0=src, in1=vbias_bc[:, :])
                else:
                    eng.tensor_copy(out=out_sb[:, ts(i, 1024)], in_=src)
            ctx.vv.inc(eng.drain())
    ctx.phase(vector=v)
    return ctx.vv.n



def emit_attention(ctx, hgs, kdram, vdram, Q, s_raw, LR, lbuf, T, kblk, vblk, P2,
                   q_ready=None, kcb=8):
    PS = ctx.PS
    flat, blocks = [], []
    for gi, hg in enumerate(hgs):
        for bi, blk in enumerate(hg["blocks"]):
            blocks.append((gi, bi, blk["tiles"], len(flat)))
            for it in blk["items"]:
                flat.append((gi, it))
    n_items = len(flat)
    if n_items == 0:
        return

    mm0, mv0, ex0, mk0, ev0 = ctx.mm.n, ctx.mv.n, ctx.ex.n, ctx.mk.n, ctx.ev.n
    sq0 = ctx.sq.n
    wl0 = ctx.wl.n
    has_mask = [bool(it.get("masks")) for (_, it) in flat]
    mask_cum = np.cumsum([1 if m else 0 for m in has_mask]).tolist()

    item_blk = []
    for bidx in range(len(blocks)):
        nxt = blocks[bidx + 1][3] if bidx + 1 < len(blocks) else n_items
        item_blk += [bidx] * (nxt - blocks[bidx][3])
    last_item_of_blk = {}
    for idx, b in enumerate(item_blk):
        last_item_of_blk[b] = idx
    item_hg = [gi for (gi, _) in flat]
    n_hg = len(hgs)

    blk_sem = [ctx.wlA if i % 2 == 0 else ctx.wlB for i in range(len(blocks))]
    wl_after_blk = []
    wl_before_blk = []
    parled = {0: ctx.wlA.n, 1: ctx.wlB.n}
    for bidx, (gi, bi, tiles, fidx) in enumerate(blocks):
        if isinstance(tiles, dict):
            nload = 1
        elif tiles and isinstance(tiles[0], (list, tuple)):
            nload = len(tiles[0])
        else:
            nload = len(tiles)
        par = bidx % 2
        wl_before_blk.append(parled[par])
        parled[par] += 32 * nload
        wl_after_blk.append(parled[par])

    def g(eng):
        for bidx, (gi, bi, tiles, fidx) in enumerate(blocks):
            par = bidx % 2
            sem = blk_sem[bidx]
            if bidx >= 2:
                eng.wait_ge(ctx.mv.h, mv0 + last_item_of_blk[bidx - 2] + 1)
            eng.wait_ge(sem.h, wl_before_blk[bidx])
            kdst = kblk[:, par * (kcb * 1024):(par + 1) * (kcb * 1024)]
            vdst = vblk[:, par * (kcb * 1040):(par + 1) * (kcb * 1040)]
            if isinstance(tiles, dict):
                L = tiles["len"]
                kg, vg = tiles["kdram_g"], tiles["vdram_g"]
                if "start" in tiles:
                    s0 = tiles["start"] + 1
                    sem.dma(eng.dma_start(
                        out=kdst.rearrange("p (f j c) -> p f (j c)", f=8, j=kcb)[:, :, 0:L * 128],
                        in_=kg[:, :, s0 * 128:(s0 + L) * 128].rearrange("f p c -> p f c")))
                    sem.dma(eng.dma_start(
                        out=vdst[:, 0:L * 1040],
                        in_=vg[s0:s0 + L, :, :].rearrange("i p w -> p i w")))
                else:
                    for arm in eng.switch_core_id(n=128):
                        s0 = tiles["starts"][arm.logical % 8] + 1
                        eng.dma_start(
                            out=kdst.rearrange("p (f j c) -> p f (j c)", f=8, j=kcb)[:, :, 0:L * 128],
                            in_=kg[:, :, s0 * 128:(s0 + L) * 128].rearrange("f p c -> p f c"),
                        ).then_inc(sem.h, 16)
                        eng.dma_start(
                            out=vdst[:, 0:L * 1040],
                            in_=vg[s0:s0 + L, :, :].rearrange("i p w -> p i w"),
                        ).then_inc(sem.h, 16)
                    sem.n += 32
            elif tiles and isinstance(tiles[0], (list, tuple)):
                nload = len(tiles[0])
                assert all(len(tl) == nload for tl in tiles)
                for arm in eng.switch_core_id(n=128):
                    tl = tiles[arm.logical % 8]
                    for si, t_ in enumerate(tl):
                        eng.dma_start(
                            out=kdst.rearrange("p (f j c) -> p f j c", f=8, j=kcb)[:, :, si, :],
                            in_=kdram[:, :, t_ * 128:(t_ + 1) * 128].rearrange("f p c -> p f c"),
                        ).then_inc(sem.h, 16)
                        eng.dma_start(
                            out=vdst[:, si * 1040:(si + 1) * 1040], in_=vdram[t_, :, :],
                        ).then_inc(sem.h, 16)
                sem.n += 32 * nload
            else:
                for si, t_ in enumerate(tiles):
                    sem.dma(eng.dma_start(
                        out=kdst.rearrange("p (f j c) -> p f j c", f=8, j=kcb)[:, :, si, :],
                        in_=kdram[:, :, t_ * 128:(t_ + 1) * 128].rearrange("f p c -> p f c")))
                    sem.dma(eng.dma_start(
                        out=vdst[:, si * 1040:(si + 1) * 1040], in_=vdram[t_, :, :]))
    ctx.phase(gpsimd=g)

    def emit_av(eng, idx, started):
        gi, it = flat[idx]
        if idx > 0 and item_hg[idx - 1] != gi:
            eng.wait_ge(ctx.ev.h, ev0 + item_hg[idx - 1] + 1)
        pb = item_blk[idx] % 2
        pvb = vblk[:, pb * (kcb * 1040):(pb + 1) * (kcb * 1040)]
        eng.wait_ge(ctx.ex.h, ex0 + idx + 1)
        par = idx % 2
        for (slot, head, p_col, width, ps_o_col) in it["av"]:
            for j0 in range(0, width, 512):
                w = min(512, width - j0)
                st_flag = (gi, (ps_o_col + j0) // 512) not in started
                started.add((gi, (ps_o_col + j0) // 512))
                eng.matmul(PS[0:65, 2048 + ps_o_col + j0: 2048 + ps_o_col + j0 + w],
                           pvb[:, slot * 1040 + head * 65: slot * 1040 + (head + 1) * 65],
                           P2[:, par * 2048 + p_col + j0: par * 2048 + p_col + j0 + w],
                           start=st_flag, stop=False, skip_group_check=True)
        ctx.mv.inc(eng.drain())

    def t(eng):
        started = set()
        if q_ready is not None:
            ctx.vv.wait(eng, q_ready)
        for idx in range(n_items):
            gi, it = flat[idx]
            bidx = item_blk[idx]
            kb = kblk[:, (bidx % 2) * (kcb * 1024):(bidx % 2 + 1) * (kcb * 1024)]
            if idx == 0 or item_blk[idx - 1] != bidx:
                eng.wait_ge(blk_sem[bidx].h, wl_after_blk[bidx])
            if idx >= 2:
                eng.wait_ge(ctx.ex.h, ex0 + idx - 1)
            sbase = (idx % 2) * 1024
            for entry in it["scores"]:
                if len(entry) == 6:
                    (slot, head, q_off, q_w, ps_col, q_ap) = entry
                else:
                    (slot, head, q_off, q_w, ps_col) = entry
                    q_ap = None
                hb = (head % 2) * 64
                fc = head // 2
                if q_ap is not None:
                    assert q_w <= 512
                    eng.matmul(PS[:, sbase + ps_col: sbase + ps_col + q_w],
                               kb[hb:hb + 64, fc * (kcb * 128) + slot * 128: fc * (kcb * 128) + (slot + 1) * 128],
                               q_ap, start=True, stop=True)
                    continue
                for j0 in range(0, q_w, 512):
                    w = min(512, q_w - j0)
                    eng.matmul(PS[:, sbase + ps_col + j0: sbase + ps_col + j0 + w],
                               kb[hb:hb + 64, fc * (kcb * 128) + slot * 128: fc * (kcb * 128) + (slot + 1) * 128],
                               Q[hb:hb + 64, fc * T + q_off + j0: fc * T + q_off + j0 + w],
                               start=True, stop=True)
            ctx.mm.inc(eng.drain())
            if idx >= 1:
                emit_av(eng, idx - 1, started)
        emit_av(eng, n_items - 1, started)
    ctx.phase(tensor=t)

    def v(eng):
        for idx in range(n_items):
            gi, it = flat[idx]
            sbase = (idx % 2) * 1024
            if it.get("masks"):
                eng.wait_ge(ctx.mm.h, mm0 + idx + 1)
                for (mask_ap, mc, mw) in it["masks"]:
                    eng.tensor_add(out=PS[:, sbase + mc: sbase + mc + mw],
                                   in0=PS[:, sbase + mc: sbase + mc + mw], in1=mask_ap)
                ctx.mk.inc(eng.drain())
            if idx + 1 == n_items or item_hg[idx + 1] != gi:
                eng.wait_ge(ctx.mv.h, mv0 + idx + 1)
                if gi >= 1:
                    nship = sum((1 if hgs[k].get("l_ship") else len(hgs[k]["evacs"]))
                                for k in range(gi))
                    eng.wait_ge(ctx.sq.h, sq0 + 16 * nship)
                hg = hgs[gi]
                for ei, (head, pso_off, pso_w, q_off) in enumerate(hg["evacs"]):
                    hb = (head % 2) * 64
                    fc = head // 2
                    dst = (hg["evac_ap"](head, q_off, pso_w) if hg.get("evac_ap")
                           else s_raw[hb:hb + 64, fc * T + q_off: fc * T + q_off + pso_w])
                    eng.tensor_copy(out=dst, in_=PS[0:64, 2048 + pso_off: 2048 + pso_off + pso_w])
                    ldst = (hg["evac_l_ap"](head, q_off, pso_w) if hg.get("evac_l_ap")
                            else LR[64:65, (ei % 2) * 1024 + q_off: (ei % 2) * 1024 + q_off + pso_w])
                    eng.reciprocal(out=ldst, in_=PS[64:65, 2048 + pso_off: 2048 + pso_off + pso_w])
                ctx.ev.inc(eng.drain())
    ctx.phase(vector=v)

    ev_b = None
    def sy(eng):
        nonlocal ev_b
        for gi, hg in enumerate(hgs):
            eng.wait_ge(ctx.ev.h, ev0 + gi + 1)
            eng.wait_ge(ctx.sq.h, ctx.sq.n)
            if hg.get("l_ship"):
                src, dst = hg["l_ship"]
                ctx.sq.dma(eng.dma_start(out=dst, in_=src))
                continue
            for ei, (head, pso_off, pso_w, q_off) in enumerate(hg["evacs"]):
                lsrc = (hg["lr_src_ap"](ei, q_off, pso_w) if hg.get("lr_src_ap")
                        else LR[64:65, (ei % 2) * 1024 + q_off: (ei % 2) * 1024 + q_off + pso_w])
                ctx.sq.dma(eng.dma_start(
                    out=lbuf[head:head + 1, q_off:q_off + pso_w], in_=lsrc))
    ctx.phase(sync=sy)

    def s(eng):
        for idx in range(n_items):
            gi, it = flat[idx]
            sbase = (idx % 2) * 1024
            eng.wait_ge(ctx.mm.h, mm0 + idx + 1)
            if has_mask[idx]:
                eng.wait_ge(ctx.mk.h, mk0 + mask_cum[idx])
            if idx >= 2:
                eng.wait_ge(ctx.mv.h, mv0 + idx - 1)
            (ec, ew, scale) = it["exp"]
            if isinstance(scale, tuple):
                kw = {"scale": scale[0][:, scale[1]:scale[1] + 1]}
            else:
                kw = {"scale": float(scale)}
            eng.activation(out=P2[:, (idx % 2) * 2048 + ec: (idx % 2) * 2048 + ec + ew],
                           in_=PS[:, sbase + ec: sbase + ec + ew], func=AF.Exp, **kw)
            ctx.ex.inc(eng.drain())
    ctx.phase(scalar=s)


def emit_normalize(ctx, s_raw, lbuf, s_bf, T, LB):
    sq_need = ctx.sq.n
    vv_base = ctx.vv.n
    ld_base = ctx.ld.n

    def sy(eng):
        for c in range(NFC):
            if c > 0:
                eng.wait_ge(ctx.vv.h, vv_base + c)
                eng.wait_ge(ctx.ld.h, ctx.ld.n)
            else:
                eng.wait_ge(ctx.sq.h, sq_need)
                eng.wait_ge(ctx.ld.h, ld_base)
            ctx.ld.dma(eng.dma_start(out=LB[0:64, 0:T],
                                     in_=lbuf[2 * c:2 * c + 1, :].broadcast_to([64, T])))
            ctx.ld.dma(eng.dma_start(out=LB[64:128, 0:T],
                                     in_=lbuf[2 * c + 1:2 * c + 2, :].broadcast_to([64, T])))
    ctx.phase(sync=sy)

    def v2(eng):
        for c in range(NFC):
            eng.wait_ge(ctx.ld.h, ld_base + 32 * (c + 1))
            eng.tensor_mul(out=s_bf[:, cs(c, T)], in0=s_raw[:, cs(c, T)], in1=LB[:, 0:T])
            ctx.vv.inc(eng.drain())
    ctx.phase(vector=v2)
    return ctx.vv.n




TA = 1024
TB = 1024
T_OWN = 2048
KCB = 8


def set_sizes(s=None, ta=None, tb=None, t_own=None, kcb=None):
    global TA, TB, T_OWN, KCB, S, NT
    if s is not None:
        S = s
        NT = s // 128
    if ta is not None:
        TA = ta
    if tb is not None:
        TB = tb
    if t_own is not None:
        T_OWN = t_own
    if kcb is not None:
        KCB = kcb


def build_common(nc, ctx, w_specs):
    d = {}
    d["x_full"] = nc.dram_tensor("x_full", [D, S], F32, kind="ExternalInput").ap()
    d["x_own"] = nc.dram_tensor("x_own", [D, T_OWN], F32, kind="ExternalInput").ap()
    d["x_next"] = nc.dram_tensor("x_next", [D, T_OWN], F32, kind="ExternalOutput").ap()
    d["kvk_g"] = nc.dram_tensor("kvk", [8, 128, S + 384], BF16, kind="Internal").ap()
    d["kvv_g"] = nc.dram_tensor("kvv", [NT + 3, 128, VW], BF16, kind="Internal").ap()
    d["kvk"] = d["kvk_g"][:, :, 128:128 + S]
    d["kvv"] = d["kvv_g"][1:NT + 1, :, :]
    d["lbuf"] = nc.dram_tensor("lbuf", [16, TB], F32, kind="Internal").ap()
    for (name, shape, dtype) in w_specs:
        d[name] = nc.dram_tensor(name, shape, dtype, kind="ExternalInput").ap()

    BASE = 16512
    def A(name, shape, dtype, offset):
        return nc.alloc_sbuf_tensor_at(name, shape, dtype, offset=BASE + offset)
    d["XF"] = A("XF", [128, 8 * 1024], F32, offset=0)
    d["KBLK"] = A("KBLK", [128, 2 * 8 * 1024], BF16, offset=0)
    d["wtmp"] = A("wtmp", [128, 8 * 1024], BF16, offset=0)
    d["XB"] = A("XB", [128, 8 * 1024], BF16, offset=32768)
    d["XN"] = A("XN", [128, 8 * 1024], BF16, offset=49152)
    d["VBLK"] = A("VBLK", [128, 2 * 8 * 1040], BF16, offset=32768)
    d["QQ"] = A("QQ", [128, 8 * 1024], BF16, offset=66048)
    d["GG"] = A("GG", [128, 8 * 1024], BF16, offset=82432)
    d["SRAW"] = A("SRAW", [128, 8 * 1024], BF16, offset=98816)
    d["VSTG"] = A("VSTG", [128, 8 * VW], BF16, offset=98816)
    d["ROWS"] = A("ROWS", [1, 4 * 1024], F32, offset=98816)
    d["SBF"] = A("SBF", [128, 8 * 1024], BF16, offset=115200)
    d["brow"] = A("brow", [1, 1024], F32, offset=115200)
    d["browb"] = A("browb", [1, 1024], BF16, offset=119296)
    d["WW"] = A("WW", [128, 8 * 3072], BF16, offset=131584)
    d["PP"] = A("PP", [128, 2 * 2048], BF16, offset=180736)
    d["LR"] = A("LR", [128, 2048], F32, offset=188928)
    d["LB"] = A("LB", [128, 1024], F32, offset=197120)
    off = 201216
    def al(name, shape, dtype):
        nonlocal off
        t = A(name, shape, dtype, offset=off)
        nbytes = int(np.prod(shape[1:])) * (2 if dtype == BF16 else 4)
        off += (nbytes + 31) // 32 * 32
        return t
    d["ones_col_b"] = al("ones_col_b", [128, 1], BF16)
    d["ones_row_b"] = al("ones_row_b", [1, 128], BF16)
    d["ones_row_f"] = al("ones_row_f", [1, 128], F32)
    d["vb_bc"] = al("vb_bc", [128, 1024], BF16)
    d["bias_sb"] = al("bias_sb", [128, 64], F32)
    d["maskA"] = al("maskA", [128, 4 * 128], BF16)
    d["maskW"] = al("maskW", [128, 3 * 896], BF16)
    d["scl8"] = al("scl8", [128, 16], F32)
    d["sigrow"] = al("sigrow", [1, 16], F32)
    d["eps_col"] = al("eps_col", [128, 1], F32)
    assert off <= 212300, off
    return d


def emit_zero_guards(ctx, d):
    def v(eng):
        last = eng.memset(d["PP"][:, 0:1040], 0.0)
        ctx.vv.inc(last)
    ctx.phase(vector=v)
    tgt = ctx.vv.n

    def g(eng):
        eng.wait_ge(ctx.vv.h, tgt)
        ctx.so.dma(eng.dma_start(
            out=d["kvk_g"][:, :, 0:128].rearrange("f p c -> p f c"),
            in_=d["PP"][:, 0:1024].rearrange("p (f c) -> p f c", f=8)))
        ctx.so.dma(eng.dma_start(
            out=d["kvk_g"][:, :, S + 128:S + 256].rearrange("f p c -> p f c"),
            in_=d["PP"][:, 0:1024].rearrange("p (f c) -> p f c", f=8)))
        ctx.so.dma(eng.dma_start(
            out=d["kvk_g"][:, :, S + 256:S + 384].rearrange("f p c -> p f c"),
            in_=d["PP"][:, 0:1024].rearrange("p (f c) -> p f c", f=8)))
        ctx.so.dma(eng.dma_start(out=d["kvv_g"][0, :, :], in_=d["PP"][:, 0:1040]))
        ctx.so.dma(eng.dma_start(out=d["kvv_g"][NT + 1, :, :], in_=d["PP"][:, 0:1040]))
        ctx.so.dma(eng.dma_start(out=d["kvv_g"][NT + 2, :, :], in_=d["PP"][:, 0:1040]))
    ctx.phase(gpsimd=g)


def ln_sc(d):
    R = d["ROWS"]
    return {"rowm": R[0:1, 0:1024], "rowv": R[0:1, 1024:2048],
            "rowa": R[0:1, 2048:3072], "rowb": R[0:1, 3072:4096],
            "xsq": d["GG"], "ones_col_b": d["ones_col_b"], "ones_row_f": d["ones_row_f"],
            "eps_col": d["eps_col"]}


def emit_ln2(ctx, d, xf, T, ready):
    sc = ln_sc(d)
    sc2 = dict(sc)
    sc2["rowm"] = sc["rowm"][:, 0:T]
    sc2["rowv"] = sc["rowv"][:, 0:T]
    sc2["rowa"] = sc["rowa"][:, 0:T]
    sc2["rowb"] = sc["rowb"][:, 0:T]
    return emit_ln(ctx, xf, d["XB"], d["XN"], T, sc2, ready_ld=ready)


def emit_phaseA(ctx, d, kproj, vproj, rope=None):
    for g in range(S // TA):
        ctx.barrier()
        def ld(eng, g=g):
            ctx.ld.dma(eng.dma_start(
                out=d["XF"][:, 0:8 * TA].rearrange("p (c t) -> p c t", c=8),
                in_=d["x_full"].rearrange("(c p) s -> p c s", p=128)[:, :, g * TA:(g + 1) * TA]))
        ctx.phase(sync=ld)
        emit_ln2(ctx, d, d["XF"], TA, (ctx.ld, ctx.ld.n))
        xn_ready = ctx.vv.n
        emit_proj_fm(ctx, d["XN"], kproj["w_sb"], TA, kproj["oc0"], 8, d["QQ"], TA, 0,
                     kproj["w_oc_total"], bias_col=kproj.get("bias_col"),
                     w_ready=kproj.get("w_ready"), xn_ready=xn_ready, bank0=4)
        if rope is not None:
            rope(ctx, d, d["QQ"], TA, g, ctx.ex.n)
            k_sem, k_done = ctx.vv, ctx.vv.n
        else:
            k_sem, k_done = ctx.ex, ctx.ex.n
        emit_proj_tok(ctx, d["XN"], vproj["w_sb"], TA, d["VSTG"], vproj["w_oc_total"],
                      vbias_bc=vproj.get("vb_bc"), w_ready=vproj.get("w_ready"),
                      xn_ready=xn_ready, bank0=6, oc0=vproj["oc0"])
        v_done = ctx.vv.n
        def stkv(eng, g=g, k_sem=k_sem, k_done=k_done, v_done=v_done):
            eng.wait_ge(k_sem.h, k_done)
            ctx.so.dma(eng.dma_start(
                out=d["kvk"][:, :, g * TA:(g + 1) * TA].rearrange("f p t -> p f t"),
                in_=d["QQ"][:, 0:8 * TA].rearrange("p (f t) -> p f t", f=8)))
            eng.wait_ge(ctx.vv.h, v_done)
            ng = TA // 128
            ctx.so.dma(eng.dma_start(
                out=d["kvv"][g * ng:(g + 1) * ng, :, :].rearrange("i p w -> p i w"),
                in_=d["VSTG"][:, 0:ng * VW].rearrange("p (i w) -> p i w", i=ng)))
        ctx.phase(gpsimd=stkv)
    def wfin(eng):
        ctx.so.wait(eng)
    ctx.phase(tensor=wfin, gpsimd=wfin, sync=wfin)


def emit_vbias_prep(ctx, d, brow_dram):
    def ld(eng):
        ctx.ld.dma(eng.dma_start(out=d["brow"][:, :], in_=brow_dram))
    ctx.phase(sync=ld)
    tgt = ctx.ld.n

    def v0(eng):
        ctx.ld.wait(eng, tgt)
        eng.tensor_copy(out=d["browb"][:, :], in_=d["brow"][:, :])
        ctx.vv.inc(eng.drain())
    ctx.phase(vector=v0)
    r = ctx.vv.n

    def t0(eng):
        ctx.vv.wait(eng, r)
        for j in range(2):
            eng.matmul(ctx.PS[:, j * 512:(j + 1) * 512], d["ones_row_b"][:, :],
                       d["browb"][0:1, j * 512:(j + 1) * 512], start=True, stop=True)
        ctx.mm.inc(eng.drain())
    ctx.phase(tensor=t0)
    r2 = ctx.mm.n

    def v1(eng):
        ctx.mm.wait(eng, r2)
        eng.tensor_copy(out=d["vb_bc"][:, :], in_=ctx.PS[:, 0:1024])
        ctx.vv.inc(eng.drain())
    ctx.phase(vector=v1)


def emit_update_store(ctx, d, p, s_src, g_src):
    def ld(eng):
        ctx.ld.dma(eng.dma_start(
            out=d["XF"][:, 0:8 * TB].rearrange("p (c t) -> p c t", c=8),
            in_=d["x_own"].rearrange("(c p) s -> p c s", p=128)[:, :, p * TB:(p + 1) * TB]))
    ctx.phase(sync=ld)
    tgt = ctx.ld.n

    def v(eng):
        ctx.ld.wait(eng, tgt)
        last = None
        for c in range(NFC):
            i1 = eng.tensor_mul(out=d["XB"][:, cs(c, TB)], in0=s_src[:, cs(c, TB)], in1=g_src[:, cs(c, TB)])
            ctx.vv.chain(eng, i1)
            last = eng.tensor_add(out=d["XF"][:, cs(c, TB)], in0=d["XF"][:, cs(c, TB)], in1=d["XB"][:, cs(c, TB)])
        ctx.vv.inc(last)
    ctx.phase(vector=v)
    done = ctx.vv.n

    def st(eng):
        eng.wait_ge(ctx.vv.h, done)
        ctx.so.dma(eng.dma_start(
            out=d["x_next"].rearrange("(c p) s -> p c s", p=128)[:, :, p * TB:(p + 1) * TB],
            in_=d["XF"][:, 0:8 * TB].rearrange("p (c t) -> p c t", c=8)))
    ctx.phase(gpsimd=st)


def emit_wload(ctx, eng_name, dst, src, nic, oc):
    def w(eng):
        eng.wait_ge(ctx.wl.h, ctx.wl.n)
        for ic in range(nic):
            ctx.wl.dma(eng.dma_start(out=dst[:, ic * oc:(ic + 1) * oc],
                                     in_=src[ts(ic), :]))
    ctx.phase(**{eng_name: w})
    return ctx.wl.n


def finish(ctx):
    def fin(eng):
        ctx.so.wait(eng)
        ctx.sq.wait(eng)
    ctx.phase(gpsimd=fin)


def full_attn_hgs(scale, T):
    hgs = []
    for hg in range(8):
        h0, h1 = 2 * hg, 2 * hg + 1
        blocks = []
        for b in range(NT // KCB):
            items = []
            for j in range(KCB):
                items.append(dict(scores=[(j, h0, 0, T, 0)], exp=(0, T, scale),
                                  av=[(j, h0, 0, T, 0)]))
                items.append(dict(scores=[(j, h1, 0, T, 0)], exp=(0, T, scale),
                                  av=[(j, h1, 0, T, 1024)]))
            blocks.append(dict(tiles=list(range(KCB * b, KCB * (b + 1))), items=items))
        hgs.append(dict(evacs=[(h0, 0, T, 0), (h1, 1024, T, 0)], blocks=blocks))
    return hgs


def build_L1():
    nc = bass.Bass("TRN2", target_bir_lowering=False, num_devices=8)
    st = contextlib.ExitStack()
    w_specs = [
        ("wqkv", [D, 3 * D], F32), ("bqkv_col", [128, 24], F32),
        ("bv_row", [1, 1024], F32), ("wout", [D, D], F32), ("bout_col", [128, 8], F32),
        ("wcb", [D, D], F32), ("scb_col", [128, 8], F32), ("bffn_col", [128, 8], F32),
    ]
    with nc.Block() as block:
        ctx = Ctx(nc, block, st)
        d = build_common(nc, ctx, w_specs)
        ctx.d = d
        emit_setup(ctx, d)
        emit_zero_guards(ctx, d)
        def wload(eng):
            eng.wait_ge(ctx.wl.h, ctx.wl.n)
            for ic in range(8):
                ctx.wl.dma(eng.dma_start(out=d["WW"][:, ic * 3072:(ic + 1) * 3072],
                                         in_=d["wqkv"][ts(ic), :]))
            ctx.wl.dma(eng.dma_start(out=d["bias_sb"][:, 0:24], in_=d["bqkv_col"]))
            ctx.wl.dma(eng.dma_start(out=d["bias_sb"][:, 24:32], in_=d["bout_col"]))
            ctx.wl.dma(eng.dma_start(out=d["bias_sb"][:, 32:40], in_=d["scb_col"]))
            ctx.wl.dma(eng.dma_start(out=d["bias_sb"][:, 40:48], in_=d["bffn_col"]))
        ctx.phase(gpsimd=wload)
        w_ready = ctx.wl.n
        emit_vbias_prep(ctx, d, d["bv_row"])

        def vscb(eng):
            ctx.wl.wait(eng, w_ready)
            eng.tensor_scalar(out=d["bias_sb"][:, 32:40], in0=d["bias_sb"][:, 32:40],
                              scalar1=1.0 / 127.0, scalar2=None, op0=OP.mult)
            ctx.vv.inc(eng.drain())
        ctx.phase(vector=vscb)

        kproj = dict(w_sb=d["WW"], oc0=1024, w_oc_total=3072,
                     bias_col=d["bias_sb"][:, 8:32], w_ready=w_ready)
        vproj = dict(w_sb=d["WW"], oc0=2048, w_oc_total=3072, vb_bc=d["vb_bc"],
                     w_ready=w_ready)
        emit_phaseA(ctx, d, kproj, vproj)

        for p in range(T_OWN // TB):
            ctx.barrier()
            def ld(eng, p=p):
                ctx.ld.dma(eng.dma_start(
                    out=d["XF"][:, 0:8 * TB].rearrange("p (c t) -> p c t", c=8),
                    in_=d["x_own"].rearrange("(c p) s -> p c s", p=128)[:, :, p * TB:(p + 1) * TB]))
            ctx.phase(sync=ld)
            emit_ln2(ctx, d, d["XF"], TB, (ctx.ld, ctx.ld.n))
            xn_ready = ctx.vv.n
            ctx.barrier()
            wr = emit_wload(ctx, "gpsimd", d["wtmp"], d["wcb"], 8, 1024)
            emit_proj_fm(ctx, d["XN"], d["wtmp"], TB, 0, 8, d["GG"], TB, 0, 1024,
                         bias_col=d["bias_sb"][:, 40:48], w_ready=wr,
                         xn_ready=xn_ready, act=AF.Silu,
                         act_scale=(d["bias_sb"], 32), bank0=4)
            emit_proj_fm(ctx, d["XN"], d["WW"], TB, 0, 8, d["QQ"], TB, 0, 3072,
                         bias_col=d["bias_sb"][:, 0:24], w_ready=w_ready,
                         xn_ready=xn_ready, bank0=6)
            ctx.barrier()
            emit_attention(ctx, full_attn_hgs(0.125, TB), d["kvk"], d["kvv"], d["QQ"],
                           d["SRAW"], d["LR"], d["lbuf"], TB, d["KBLK"], d["VBLK"], d["PP"])
            ctx.barrier()
            emit_normalize(ctx, d["SRAW"], d["lbuf"], d["SBF"], TB, d["LB"])
            ctx.barrier()
            wr = emit_wload(ctx, "gpsimd", d["wtmp"], d["wout"], 8, 1024)
            emit_proj_fm(ctx, d["SBF"], d["wtmp"], TB, 0, 8, d["SRAW"], TB, 0, 1024,
                         bias_col=d["bias_sb"][:, 24:32], w_ready=wr, bank0=4)
            ctx.barrier()
            emit_update_store(ctx, d, p, d["SRAW"], d["GG"])
        finish(ctx)
    st.close()
    return nc



def emit_rope(ctx, d, Qt, T, cos_dram_off, cos_dram, sin_dram):
    COS = d["LR"][:, 0:1024]
    SIN = d["LR"][:, 1024:2048]
    def ld(eng):
        eng.wait_ge(ctx.ld.h, ctx.ld.n)
        ctx.ld.dma(eng.dma_start(out=COS[:, 0:T], in_=cos_dram[:, cos_dram_off:cos_dram_off + T]))
        ctx.ld.dma(eng.dma_start(out=SIN[:, 0:T], in_=sin_dram[:, cos_dram_off:cos_dram_off + T]))
    ctx.phase(sync=ld)
    ld_tgt = ctx.ld.n

    vv0 = ctx.vv.n
    mm0 = ctx.mm.n
    ex_need = ctx.ex.n
    def v1(eng):
        eng.wait_ge(ctx.ld.h, ld_tgt)
        eng.wait_ge(ctx.ex.h, ex_need)
        for c in range(NFC):
            if c > 0:
                eng.wait_ge(ctx.mm.h, mm0 + c)
            i1 = eng.tensor_mul(out=d["XB"][:, 0:T], in0=Qt[:, cs(c, T)], in1=SIN[:, 0:T])
            ctx.vv.chain(eng, i1)
            eng.wait_ge(ctx.mm.h, mm0 + c + 1)
            i2 = eng.tensor_mul(out=d["XB"][:, T:2 * T], in0=Qt[:, cs(c, T)], in1=COS[:, 0:T])
            ctx.vv.chain(eng, i2)
            i3 = eng.tensor_add(out=Qt[:, cs(c, T)], in0=d["XB"][:, T:2 * T], in1=ctx.PS[:, 0:T])
            ctx.vv.inc(i3)
    def t1(eng):
        for c in range(NFC):
            eng.wait_ge(ctx.vv.h, vv0 + 3 * c + 1)
            if c > 0:
                eng.wait_ge(ctx.vv.h, vv0 + 3 * c)
            for hb in (0, 64):
                for j in range(0, T, 512):
                    w = min(512, T - j)
                    eng.matmul(ctx.PS[hb:hb + 64, j:j + w] if hb == 0 else ctx.PS[64:128, j:j + w],
                               d["RT2"][hb:hb + 64, :], d["XB"][hb:hb + 64, j:j + w],
                               start=True, stop=True)
            ctx.mm.inc(eng.drain())
    ctx.phase(vector=v1, tensor=t1)
    return ctx.vv.n


def l0_attn_hgs(d, p, npass_tiles=None):
    if npass_tiles is None:
        npass_tiles = TB // 128
    t0 = npass_tiles * p
    hgs = []
    for hg in range(8):
        h0, h1 = 2 * hg, 2 * hg + 1
        blocks = []
        bulk = list(range(0, 4 * t0))
        band = list(range(4 * t0, 4 * t0 + 4 * npass_tiles))
        for b0 in range(0, len(bulk), KCB):
            tl = bulk[b0:b0 + KCB]
            items = []
            for si, kc in enumerate(tl):
                for (h, pso) in ((h0, 0), (h1, 1024)):
                    items.append(dict(scores=[(si, h, 0, TB, 0)], exp=(0, TB, 0.125),
                                      av=[(si, h, 0, TB, pso)]))
            blocks.append(dict(tiles=tl, items=items))
        for b0 in range(0, len(band), KCB):
            tl = band[b0:b0 + KCB]
            items = []
            for si, kc in enumerate(tl):
                a = (kc - 4 * t0) // 4
                b = kc % 4
                q_off = a * 128
                q_w = TB - q_off
                msk = d["maskA"][:, (b * 128):(b * 128 + 128)]
                for (h, pso) in ((h0, 0), (h1, 1024)):
                    items.append(dict(
                        scores=[(si, h, q_off, q_w, 0)],
                        exp=(0, q_w, 0.125),
                        masks=[(msk, 0, 128)],
                        av=[(si, h, 0, q_w, pso + q_off)]))
            blocks.append(dict(tiles=tl, items=items))
        hgs.append(dict(evacs=[(h0, 0, TB, 0), (h1, 1024, TB, 0)], blocks=blocks))
    return hgs


def emit_build_M(ctx, d, w_ready):
    WTR = d["WTR"]; WTI = d["WTI"]; TMP = d["WTT"]
    F1S = d["F1S"]; F2S = d["F2S"]
    vv0 = ctx.vv.n
    ld0 = ctx.ld.n
    def ld(eng):
        for pi, (f1, f2) in enumerate((("F1R", "F2R"), ("F1I", "F2I"))):
            for c in range(4):
                for rc in range(4):
                    eng.wait_ge(ctx.ld.h, ctx.ld.n)
                    if not (pi == 0 and c == 0 and rc == 0):
                        eng.wait_ge(ctx.vv.h, vv0 + 8 * pi + ... if False else ctx.vv.n)
                    ctx.ld.dma(eng.dma_start(out=F1S[:, :], in_=d[f1][c, rc * 128:(rc + 1) * 128, :]))
                    ctx.ld.dma(eng.dma_start(out=F2S[:, :], in_=d[f2][c, rc * 128:(rc + 1) * 128, :]))
    seq = []
    for pi in range(2):
        for rc in range(4):
            for c in range(4):
                seq.append((pi, rc, c))
    vv_cum = []
    acc = 0
    for (pi, rc, c) in seq:
        acc += 1 if c == 0 else 2
        vv_cum.append(acc)

    def ld2(eng):
        for i, (pi, rc, c) in enumerate(seq):
            eng.wait_ge(ctx.ld.h, ld0 + 32 * i)
            if i > 0:
                eng.wait_ge(ctx.vv.h, vv0 + vv_cum[i - 1])
            f1 = "F1R" if pi == 0 else "F1I"
            f2 = "F2R" if pi == 0 else "F2I"
            ctx.ld.dma(eng.dma_start(out=F1S[:, :], in_=d[f1][c, rc * 128:(rc + 1) * 128, :]))
            ctx.ld.dma(eng.dma_start(out=F2S[:, :], in_=d[f2][c, rc * 128:(rc + 1) * 128, :]))
    ctx.phase(sync=ld2)

    def v1(eng):
        for i, (pi, rc, c) in enumerate(seq):
            eng.wait_ge(ctx.ld.h, ld0 + 32 * (i + 1))
            acc = (WTR if pi == 0 else WTI)[:, rc * 512:(rc + 1) * 512]
            if c == 0:
                last = eng.tensor_mul(out=acc, in0=F1S[:, :], in1=F2S[:, :])
            else:
                i1 = eng.tensor_mul(out=TMP[:, :], in0=F1S[:, :], in1=F2S[:, :])
                ctx.vv.chain(eng, i1)
                last = eng.tensor_add(out=acc, in0=acc, in1=TMP[:, :])
            ctx.vv.inc(last)
    ctx.phase(vector=v1)
    built = ctx.vv.n

    def v2(eng):
        eng.wait_ge(ctx.vv.h, built)
        last = None
        for ic in range(8):
            dst = d["WW"][:, ic * 3072 + 2048: ic * 3072 + 3072]
            if ic < 4:
                eng.tensor_copy(out=dst[:, 0:512], in_=WTR[:, ic * 512:(ic + 1) * 512])
                last = eng.tensor_copy(out=dst[:, 512:1024], in_=WTI[:, ic * 512:(ic + 1) * 512])
            else:
                eng.tensor_scalar(out=dst[:, 0:512], in0=WTI[:, (ic - 4) * 512:(ic - 3) * 512],
                                  scalar1=-1.0, scalar2=None, op0=OP.mult)
                last = eng.tensor_copy(out=dst[:, 512:1024], in_=WTR[:, (ic - 4) * 512:(ic - 3) * 512])
        ctx.vv.inc(last)
    ctx.phase(vector=v2)
    return ctx.vv.n


DBG_L0 = False


def build_L0():
    nc = bass.Bass("TRN2", target_bir_lowering=False, num_devices=8)
    st = contextlib.ExitStack()
    w_specs = [
        ("wq", [D, D], F32), ("wk", [D, D], F32), ("wv", [D, D], F32), ("wo", [D, D], F32),
        ("bq_col", [128, 8], F32), ("bk_col", [128, 8], F32), ("bo_col", [128, 8], F32),
        ("bv_row", [1, 1024], F32), ("bffn_col", [128, 8], F32),
        ("F1R", [4, 512, 512], F32), ("F2R", [4, 512, 512], F32),
        ("F1I", [4, 512, 512], F32), ("F2I", [4, 512, 512], F32),
        ("cos_all", [128, S], F32), ("sin_all", [128, S], F32),
        ("cos_own", [128, T_OWN], F32), ("sin_own", [128, T_OWN], F32),
        ("rt2", [128, 64], F32),
        ("maskA_in", [128, 4 * 128], F32),
    ]
    dbg_specs = []
    if DBG_L0:
        dbg_specs = [("dbg_xn", [128, 8192]), ("dbg_gg", [128, 8192]),
                     ("dbg_qq", [128, 8192]), ("dbg_sraw", [128, 8192]),
                     ("dbg_sbf", [128, 8192]), ("dbg_kg", [128, 8192]),
                     ("dbg_vg", [128, 8 * VW])]
    with nc.Block() as block:
        ctx = Ctx(nc, block, st)
        d = build_common(nc, ctx, w_specs)
        d["WTR"] = nc.alloc_sbuf_tensor_at("WTR", [128, 2048], F32, offset=16512 + 0)
        d["WTI"] = nc.alloc_sbuf_tensor_at("WTI", [128, 2048], F32, offset=16512 + 8192)
        d["WTT"] = nc.alloc_sbuf_tensor_at("WTT", [128, 512], F32, offset=16512 + 16384)
        d["F1S"] = nc.alloc_sbuf_tensor_at("F1S", [128, 512], F32, offset=16512 + 18432)
        d["F2S"] = nc.alloc_sbuf_tensor_at("F2S", [128, 512], F32, offset=16512 + 20480)
        d["RT2"] = nc.alloc_sbuf_tensor_at("RT2", [128, 64], BF16, offset=16512 + 206560)
        ctx.d = d
        emit_setup(ctx, d)
        emit_zero_guards(ctx, d)
        def wload(eng):
            eng.wait_ge(ctx.wl.h, ctx.wl.n)
            for ic in range(8):
                ctx.wl.dma(eng.dma_start(out=d["WW"][:, ic * 3072:ic * 3072 + 1024],
                                         in_=d["wk"][ts(ic), :]))
                ctx.wl.dma(eng.dma_start(out=d["WW"][:, ic * 3072 + 1024:ic * 3072 + 2048],
                                         in_=d["wv"][ts(ic), :]))
            ctx.wl.dma(eng.dma_start(out=d["bias_sb"][:, 0:8], in_=d["bq_col"]))
            ctx.wl.dma(eng.dma_start(out=d["bias_sb"][:, 8:16], in_=d["bk_col"]))
            ctx.wl.dma(eng.dma_start(out=d["bias_sb"][:, 16:24], in_=d["bo_col"]))
            ctx.wl.dma(eng.dma_start(out=d["bias_sb"][:, 24:32], in_=d["bffn_col"]))
            ctx.wl.dma(eng.dma_start(out=d["RT2"][:, :], in_=d["rt2"]))
            ctx.wl.dma(eng.dma_start(out=d["maskA"][:, :], in_=d["maskA_in"]))
        ctx.phase(gpsimd=wload)
        w_ready = ctx.wl.n
        emit_vbias_prep(ctx, d, d["bv_row"])
        emit_build_M(ctx, d, w_ready)
        ctx.barrier()

        kproj = dict(w_sb=d["WW"], oc0=0, w_oc_total=3072,
                     bias_col=d["bias_sb"][:, 8:16], w_ready=w_ready)
        vproj = dict(w_sb=d["WW"], oc0=1024, w_oc_total=3072, vb_bc=d["vb_bc"],
                     w_ready=w_ready)

        def rope_k(ctx_, d_, Qt, T, g, kready):
            emit_rope(ctx_, d_, Qt, T, g * TA, d_["cos_all"], d_["sin_all"])
        dbgt = {}
        if DBG_L0:
            for nm, shape in dbg_specs:
                dbgt[nm] = nc.dram_tensor(nm, shape, BF16, kind="ExternalOutput").ap()

        def dump(nm, src):
            if not DBG_L0:
                return
            ctx.barrier()
            def g_(eng, nm=nm, src=src):
                ctx.so.dma(eng.dma_start(out=dbgt[nm], in_=src))
            ctx.phase(gpsimd=g_)
            ctx.barrier()
        emit_phaseA(ctx, d, kproj, vproj, rope=rope_k)

        def wload2(eng):
            eng.wait_ge(ctx.wl.h, ctx.wl.n)
            for ic in range(8):
                ctx.wl.dma(eng.dma_start(out=d["WW"][:, ic * 3072:ic * 3072 + 1024],
                                         in_=d["wq"][ts(ic), :]))
                ctx.wl.dma(eng.dma_start(out=d["WW"][:, ic * 3072 + 1024:ic * 3072 + 2048],
                                         in_=d["wo"][ts(ic), :]))
        ctx.phase(gpsimd=wload2)
        w_ready2 = ctx.wl.n

        for p in range(T_OWN // TB):
            ctx.barrier()
            def ld(eng, p=p):
                ctx.ld.dma(eng.dma_start(
                    out=d["XF"][:, 0:8 * TB].rearrange("p (c t) -> p c t", c=8),
                    in_=d["x_own"].rearrange("(c p) s -> p c s", p=128)[:, :, p * TB:(p + 1) * TB]))
            ctx.phase(sync=ld)
            emit_ln2(ctx, d, d["XF"], TB, (ctx.ld, ctx.ld.n))
            xn_ready = ctx.vv.n
            ctx.barrier()
            if p == 0:
                dump("dbg_xn", d["XN"][:, :])
            emit_proj_fm(ctx, d["XN"], d["WW"], TB, 2048, 8, d["GG"], TB, 0, 3072,
                         bias_col=d["bias_sb"][:, 24:32], w_ready=w_ready,
                         xn_ready=xn_ready, act=AF.Silu, bank0=4)
            emit_proj_fm(ctx, d["XN"], d["WW"], TB, 0, 8, d["QQ"], TB, 0, 3072,
                         bias_col=d["bias_sb"][:, 0:8], w_ready=w_ready2,
                         xn_ready=xn_ready, bank0=6)
            ctx.barrier()
            if p == 0:
                dump("dbg_gg", d["GG"][:, :])
            emit_rope(ctx, d, d["QQ"], TB, p * TB, d["cos_own"], d["sin_own"])
            ctx.barrier()
            if p == 0:
                dump("dbg_qq", d["QQ"][:, :])
            emit_attention(ctx, l0_attn_hgs(d, p), d["kvk"], d["kvv"], d["QQ"],
                           d["SRAW"], d["LR"], d["lbuf"], TB, d["KBLK"], d["VBLK"], d["PP"])
            ctx.barrier()
            if p == 0:
                dump("dbg_sraw", d["SRAW"][:, :])
            emit_normalize(ctx, d["SRAW"], d["lbuf"], d["SBF"], TB, d["LB"])
            ctx.barrier()
            if p == 0:
                dump("dbg_sbf", d["SBF"][:, :])
            emit_proj_fm(ctx, d["SBF"], d["WW"], TB, 1024, 8, d["SRAW"], TB, 0, 3072,
                         bias_col=d["bias_sb"][:, 16:24], w_ready=w_ready2, bank0=4)
            ctx.barrier()
            emit_update_store(ctx, d, p, d["SRAW"], d["GG"])
        finish(ctx)
    st.close()
    return nc



def l2_attn_hgs(d, p):
    ntiles = TB // 128
    lastp = T_OWN // TB - 1
    hgs = []
    for hg in range(8):
        h0, h1 = 2 * hg, 2 * hg + 1
        blocks = []
        for j in range(ntiles):
            start = 4 * (ntiles * p + j) - 1
            rng = dict(start=start, len=7, kdram_g=d["kvk_g"], vdram_g=d["kvv_g"])
            if p == 0 and j == 0:
                moff = 896
            elif p == lastp and j == ntiles - 1:
                moff = 2 * 896
            else:
                moff = 0
            items = []
            for h in (h0, h1):
                scores = [(s, h, j * 128, 128, s * 128) for s in range(7)]
                avs = [(s, h, s * 128, 128, (h % 2) * 1024 + j * 128) for s in range(7)]
                items.append(dict(scores=scores,
                                  exp=(0, 896, 1.0 / 32.0),
                                  masks=[(d["maskW"][:, moff:moff + 896], 0, 896)],
                                  av=avs))
            blocks.append(dict(tiles=rng, items=items))
        hgs.append(dict(evacs=[(h0, 0, TB, 0), (h1, 1024, TB, 0)], blocks=blocks))
    return hgs


def emit_l2_ffn(ctx, d, xn_ready):
    ga = d["GACC"]
    hsblk = d["XB"]
    wg = d["KBLK"]
    wd = d["QQ"]
    T = TB
    vv0 = ctx.vv.n

    def vz(eng):
        last = eng.memset(ga[:, :], 0.0)
        ctx.vv.inc(last)
    ctx.phase(vector=vz)

    for hcb in range(3):
        wl0 = ctx.wl.n
        def wld(eng, hcb=hcb):
            eng.wait_ge(ctx.wl.h, ctx.wl.n)
            if hcb > 0:
                eng.wait_ge(ctx.mm.h, ctx.mm.n)
            for ic in range(8):
                ctx.wl.dma(eng.dma_start(out=wg[:, ic * 1024:(ic + 1) * 1024],
                                         in_=d["wg_t"][ts(ic), hcb * 1024:(hcb + 1) * 1024]))
                ctx.wl.dma(eng.dma_start(out=wg[:, 8192 + ic * 1024:8192 + (ic + 1) * 1024],
                                         in_=d["wu_t"][ts(ic), hcb * 1024:(hcb + 1) * 1024]))
            for hr in range(8):
                ctx.wl.dma(eng.dma_start(out=wd[:, hr * 1024:(hr + 1) * 1024],
                                         in_=d["wd_t"][ts(hcb * 8 + hr), :]))
        ctx.phase(gpsimd=wld)
        w_ready = ctx.wl.n

        mm0 = ctx.mm.n
        ex0 = ctx.ex.n
        vvh0 = ctx.vv.n
        def th(eng, hcb=hcb, mm0=mm0, ex0=ex0, vvh0=vvh0, w_ready=w_ready):
            ctx.wl.wait(eng, w_ready)
            if xn_ready is not None:
                ctx.vv.wait(eng, xn_ready)
            for hc in range(8):
                if hc > 0:
                    eng.wait_ge(ctx.ex.h, ex0 + hc)
                    eng.wait_ge(ctx.vv.h, vvh0 + 2 * hc)
                for ic in range(8):
                    eng.matmul(ctx.PS[:, 0:TB], wg[:, ic * 1024 + hc * 128: ic * 1024 + (hc + 1) * 128],
                               d["XN"][:, ic * T: ic * T + T], start=(ic == 0), stop=(ic == 7))
                ctx.mm.inc(eng.drain())
                for ic in range(8):
                    eng.matmul(ctx.PS[:, 512:512 + TB], wg[:, 8192 + ic * 1024 + hc * 128: 8192 + ic * 1024 + (hc + 1) * 128],
                               d["XN"][:, ic * T: ic * T + T], start=(ic == 0), stop=(ic == 7))
                ctx.mm.inc(eng.drain())
        def sh(eng, mm0=mm0, ex0=ex0, vvh0=vvh0):
            for hc in range(8):
                eng.wait_ge(ctx.mm.h, mm0 + 2 * hc + 1)
                eng.activation(out=hsblk[:, 8 * TB:9 * TB], in_=ctx.PS[:, 0:TB],
                               func=AF.Silu, scale=1.0 / 127.0)
                ctx.ex.inc(eng.drain())
        def vh(eng, mm0=mm0, ex0=ex0, vvh0=vvh0):
            for hc in range(8):
                eng.wait_ge(ctx.mm.h, mm0 + 2 * hc + 2)
                i1 = eng.tensor_scalar(out=hsblk[:, 9 * TB:10 * TB], in0=ctx.PS[:, 512:512 + TB],
                                       scalar1=1.0 / 127.0, scalar2=None, op0=OP.mult)
                ctx.vv.chain(eng, i1)
                eng.wait_ge(ctx.ex.h, ex0 + hc + 1)
                i2 = eng.tensor_mul(out=hsblk[:, hc * TB:(hc + 1) * TB],
                                    in0=hsblk[:, 8 * TB:9 * TB], in1=hsblk[:, 9 * TB:10 * TB])
                ctx.vv.inc(i2)
        ctx.phase(tensor=th, scalar=sh, vector=vh)
        h_done_mm = ctx.mm.n
        h_done_vv = ctx.vv.n

        mm1 = ctx.mm.n
        vv1 = ctx.vv.n
        def to(eng, mm1=mm1, vv1=vv1, h_done_vv=h_done_vv):
            eng.wait_ge(ctx.vv.h, h_done_vv)
            n = 0
            for dcg in range(2):
                for dc in range(4):
                    j = dcg * 4 + dc
                    if dcg > 0:
                        eng.wait_ge(ctx.vv.h, vv1 + 4 * dcg)
                    for hc in range(8):
                        eng.matmul(ctx.PS[:, (2 + dc) * 512:(2 + dc) * 512 + TB],
                                   wd[:, hc * 1024 + j * 128: hc * 1024 + (j + 1) * 128],
                                   hsblk[:, hc * TB:(hc + 1) * TB],
                                   start=(hc == 0), stop=(hc == 7))
                    ctx.mm.inc(eng.drain())
        def vo(eng, mm1=mm1, vv1=vv1):
            for dcg in range(2):
                for dc in range(4):
                    j = dcg * 4 + dc
                    eng.wait_ge(ctx.mm.h, mm1 + 4 * dcg + dc + 1)
                    last = eng.scalar_tensor_tensor(
                        out=ga[:, j * TB:(j + 1) * TB],
                        in0=ctx.PS[:, (2 + dc) * 512:(2 + dc) * 512 + TB],
                        scalar=1.0 / 127.0, in1=ga[:, j * TB:(j + 1) * TB],
                        op0=OP.mult, op1=OP.add)
                    ctx.vv.inc(last)
        ctx.phase(tensor=to, vector=vo)

    fin_vv = ctx.vv.n
    def sf(eng):
        eng.wait_ge(ctx.vv.h, fin_vv)
        for j in range(8):
            eng.activation(out=d["GG"][:, j * TB:(j + 1) * TB],
                           in_=ga[:, j * TB:(j + 1) * TB], func=AF.Silu)
        ctx.ex.inc(eng.drain())
    ctx.phase(scalar=sf)
    return ctx.ex.n


def build_L2():
    nc = bass.Bass("TRN2", target_bir_lowering=False, num_devices=8)
    st = contextlib.ExitStack()
    w_specs = [
        ("wqkv", [D, 3 * D], F32), ("bqkv_col", [128, 24], F32),
        ("bv_row", [1, 1024], F32), ("wout", [D, D], F32), ("bout_col", [128, 8], F32),
        ("wg_t", [D, 3072], I32), ("wu_t", [D, 3072], I32), ("wd_t", [3072, D], I32),
        ("maskW_in", [128, 3 * 896], F32),
    ]
    with nc.Block() as block:
        ctx = Ctx(nc, block, st)
        d = build_common(nc, ctx, w_specs)
        d["GACC"] = nc.alloc_sbuf_tensor_at("GACC", [128, 8 * TB], F32, offset=16512 + 115200)
        ctx.d = d
        emit_setup(ctx, d)
        emit_zero_guards(ctx, d)
        def wload(eng):
            eng.wait_ge(ctx.wl.h, ctx.wl.n)
            for ic in range(8):
                ctx.wl.dma(eng.dma_start(out=d["WW"][:, ic * 3072:(ic + 1) * 3072],
                                         in_=d["wqkv"][ts(ic), :]))
            ctx.wl.dma(eng.dma_start(out=d["bias_sb"][:, 0:24], in_=d["bqkv_col"]))
            ctx.wl.dma(eng.dma_start(out=d["bias_sb"][:, 24:32], in_=d["bout_col"]))
            ctx.wl.dma(eng.dma_start(out=d["maskW"][:, :], in_=d["maskW_in"]))
        ctx.phase(gpsimd=wload)
        w_ready = ctx.wl.n
        emit_vbias_prep(ctx, d, d["bv_row"])

        kproj = dict(w_sb=d["WW"], oc0=1024, w_oc_total=3072,
                     bias_col=d["bias_sb"][:, 8:24], w_ready=w_ready)
        vproj = dict(w_sb=d["WW"], oc0=2048, w_oc_total=3072, vb_bc=d["vb_bc"],
                     w_ready=w_ready)
        emit_phaseA(ctx, d, kproj, vproj)

        for p in range(T_OWN // TB):
            ctx.barrier()
            def ld(eng, p=p):
                ctx.ld.dma(eng.dma_start(
                    out=d["XF"][:, 0:8 * TB].rearrange("p (c t) -> p c t", c=8),
                    in_=d["x_own"].rearrange("(c p) s -> p c s", p=128)[:, :, p * TB:(p + 1) * TB]))
            ctx.phase(sync=ld)
            emit_ln2(ctx, d, d["XF"], TB, (ctx.ld, ctx.ld.n))
            xn_ready = ctx.vv.n
            ctx.barrier()
            emit_l2_ffn(ctx, d, xn_ready)
            ctx.barrier()
            emit_proj_fm(ctx, d["XN"], d["WW"], TB, 0, 8, d["QQ"], TB, 0, 3072,
                         bias_col=d["bias_sb"][:, 0:24], w_ready=w_ready,
                         xn_ready=xn_ready, bank0=6)
            ctx.barrier()
            emit_attention(ctx, l2_attn_hgs(d, p), d["kvk"], d["kvv"], d["QQ"],
                           d["SRAW"], d["LR"], d["lbuf"], TB, d["KBLK"], d["VBLK"], d["PP"])
            ctx.barrier()
            emit_normalize(ctx, d["SRAW"], d["lbuf"], d["SBF"], TB, d["LB"])
            ctx.barrier()
            wr = emit_wload(ctx, "gpsimd", d["wtmp"], d["wout"], 8, 1024)
            emit_proj_fm(ctx, d["SBF"], d["wtmp"], TB, 0, 8, d["SRAW"], TB, 0, 1024,
                         bias_col=d["bias_sb"][:, 24:32], w_ready=wr, bank0=4)
            ctx.barrier()
            emit_update_store(ctx, d, p, d["SRAW"], d["GG"])
        finish(ctx)
    st.close()
    return nc



def l3_class_qap(d, src, hb, fc, T, c, width64=False):
    base = src[hb:hb + 64, fc * T:(fc + 1) * T]
    v = base.rearrange("p (i half cc rr) -> p i half cc rr", half=2, cc=8, rr=8)
    return v[:, :, :, c, :]


def l3_attn_hgs(d, T):
    ncls = T // 8 // 16 * 16
    span = T // 8
    hgs = []
    for h in range(16):
        hb = (h % 2) * 64
        fc = h // 2
        pso_base = (h % 2) * 1024
        blocks = []
        for b in range(NT // KCB):
            items = []
            for c in range(8):
                scores = []
                avs = []
                for j in range(KCB):
                    scores.append((j, h, 0, span, j * span,
                                   l3_class_qap(d, d["QQ"], hb, fc, T, c)))
                    avs.append((j, h, j * span, span, pso_base + c * span))
                items.append(dict(scores=scores, exp=(0, KCB * span, (d["scl8"], c)),
                                  av=avs))
            blocks.append(dict(tiles=list(range(KCB * b, KCB * (b + 1))), items=items))

        def mk_evac_ap(h=h, hb=hb, fc=fc, T=T):
            def f(head, c, w):
                return l3_class_qap(d, d["SRAW"], hb, fc, T, c)
            return f

        def mk_lr_ap(T=T, span=span):
            def f(ei, c, w):
                return d["LR"][64:65, c * span:(c + 1) * span]
            return f

        def mk_lsrc_ap(T=T, span=span):
            return mk_lr_ap()

        def mk_ldst_ap(h=h, T=T, span=span):
            def f(head, c, w):
                v = d["lbuf"][h:h + 1, :].rearrange("p (i half cc rr) -> p i half cc rr",
                                                    half=2, cc=8, rr=8)
                return v[:, :, :, c, :]
            return f
        ldst = d["lbuf"][h:h + 1, :].rearrange("p (i half cc rr) -> p i half cc rr",
                                               half=2, cc=8, rr=8).rearrange(
                                               "p i half cc rr -> p cc i half rr")
        hgs.append(dict(
            evacs=[(h, pso_base + c * span, span, c) for c in range(8)],
            evac_ap=mk_evac_ap(),
            evac_l_ap=mk_lr_ap(),
            l_ship=(d["LR"][64:65, 0:T], ldst),
            blocks=blocks))
    return hgs


def emit_l3_gating(ctx, d, xn_ready, T):
    XG = d["XBF32"]
    GSB = XG[0:8, 0:1024]
    mm0 = ctx.mm.n
    gw_ready = d["_gw_ready"]

    def t1(eng):
        ctx.wl.wait(eng, gw_ready)
        if xn_ready is not None:
            ctx.vv.wait(eng, xn_ready)
        for ic in range(8):
            for j in range(0, T, 512):
                w = min(512, T - j)
                eng.matmul(ctx.PS[0:8, j:j + w], d["gatew_sb"][:, ic * 8:(ic + 1) * 8],
                           d["XN"][:, ic * T + j: ic * T + j + w],
                           start=(ic == 0), stop=(ic == 7))
        ctx.mm.inc(eng.drain())
    ctx.phase(tensor=t1)
    g_done = ctx.mm.n

    def v1(eng):
        eng.wait_ge(ctx.mm.h, g_done)
        last = eng.tensor_scalar(out=GSB[:, 0:T], in0=ctx.PS[0:8, 0:T],
                                 scalar1=d["gateb_sb"][0:8, 0:1], scalar2=None, op0=OP.add)
        ctx.vv.inc(last)
    ctx.phase(vector=v1)
    gsb_done = ctx.vv.n

    ntile = T // 128
    mm1 = ctx.mm.n
    vv1 = ctx.vv.n
    ex1 = ctx.ex.n

    def t2(eng):
        eng.wait_ge(ctx.vv.h, gsb_done)
        for i in range(ntile):
            if i > 0:
                eng.wait_ge(ctx.vv.h, vv1 + 12 * (i - 1) + 1)
            eng.transpose(ctx.PS[0:128, 1024:1032], GSB[:, i * 128:(i + 1) * 128],
                          d["idnf_sb"][0:8, 0:8])
            ctx.mm.inc(eng.drain())
    ctx.phase(tensor=t2)

    def v2(eng):
        for i in range(ntile):
            eng.wait_ge(ctx.mm.h, mm1 + i + 1)
            Gt = XG[:, 1024:1032]
            P_ = XG[:, 1032:1040]
            r1 = XG[:, 1040:1041]
            i1 = eng.tensor_copy(out=Gt, in_=ctx.PS[0:128, 1024:1032])
            ctx.vv.chain(eng, i1)
            i2 = eng.reduce_max(out=r1, in_=Gt, axis=AX.X)
            ctx.vv.chain(eng, i2)
            i3 = eng.tensor_scalar(out=Gt, in0=Gt, scalar1=r1[:, 0:1], scalar2=None, op0=OP.subtract)
            ctx.vv.chain(eng, i3)
            eng.wait_ge(ctx.ex.h, ex1 + i + 1)
            i4 = eng.reduce_sum(out=r1, in_=P_, axis=AX.X)
            ctx.vv.chain(eng, i4)
            i5 = eng.reciprocal(out=r1, in_=r1)
            ctx.vv.chain(eng, i5)
            i6 = eng.tensor_scalar(out=P_, in0=P_, scalar1=r1[:, 0:1], scalar2=None, op0=OP.mult)
            ctx.vv.chain(eng, i6)
            i7 = eng.reduce_max(out=r1, in_=P_, axis=AX.X)
            ctx.vv.chain(eng, i7)
            m2 = XG[:, 1041:1042]
            eq = XG[:, 1042:1050]
            i8 = eng.tensor_scalar(out=eq, in0=P_, scalar1=r1[:, 0:1], scalar2=None, op0=OP.is_ge)
            ctx.vv.chain(eng, i8)
            i9 = eng.scalar_tensor_tensor(out=eq, in0=eq, scalar=BIG_NEG, in1=P_,
                                          op0=OP.mult, op1=OP.add)
            ctx.vv.chain(eng, i9)
            i10 = eng.reduce_max(out=m2, in_=eq, axis=AX.X)
            ctx.vv.chain(eng, i10)
            i11 = eng.tensor_scalar(out=eq, in0=P_, scalar1=m2[:, 0:1], scalar2=None, op0=OP.is_ge)
            ctx.vv.chain(eng, i11)
            i12 = eng.tensor_mul(out=XG[:, 1050 + 8 * i:1058 + 8 * i], in0=P_, in1=eq)
            ctx.vv.inc(i12)
    ctx.phase(vector=v2)

    def s2(eng):
        for i in range(ntile):
            eng.wait_ge(ctx.vv.h, vv1 + 12 * i + 3)
            eng.activation(out=XG[:, 1032:1040], in_=XG[:, 1024:1032], func=AF.Exp)
            ctx.ex.inc(eng.drain())
    ctx.phase(scalar=s2)
    v2_done = ctx.vv.n

    mm3 = ctx.mm.n
    vv3 = ctx.vv.n

    def t3(eng):
        eng.wait_ge(ctx.vv.h, v2_done)
        n = 0
        for i in range(ntile):
            for e in range(8):
                if n > 0:
                    eng.wait_ge(ctx.vv.h, vv3 + n)
                eng.transpose(ctx.PS[0:1, 1024:1152], XG[:, 1050 + 8 * i + e:1051 + 8 * i + e],
                              d["idnf_sb"][:, :])
                ctx.mm.inc(eng.drain())
                n += 1
    def v3(eng):
        n = 0
        for i in range(ntile):
            for e in range(8):
                eng.wait_ge(ctx.mm.h, mm3 + n + 1)
                last = eng.tensor_copy(out=d["MROW"][0:1, e * T + i * 128:e * T + (i + 1) * 128],
                                       in_=ctx.PS[0:1, 1024:1152])
                ctx.vv.inc(last)
                n += 1
    ctx.phase(tensor=t3, vector=v3)
    return ctx.vv.n


def emit_l3_moe(ctx, d, T):
    SUB = min(512, T)
    nsub = T // SUB
    NHC = 4096 // 128
    HCB = 8
    ga = d["GACC3"]
    Hs = d["HS3"]
    w1b = d["W1B"]
    w2b = d["W2B"]
    xne = d["XNE"]
    mbc = d["MBC"]
    ntt = SUB // 128

    for sub in range(nsub):
        def vz(eng):
            last = eng.memset(ga[:, 0:ntt * 1024], 0.0)
            ctx.vv.inc(last)
        ctx.phase(vector=vz)
        ctx.barrier()

        for e in range(8):
            mm0 = ctx.mm.n
            def tb(eng, e=e, sub=sub):
                for j in range(0, SUB, 512):
                    w = min(512, SUB - j)
                    eng.matmul(ctx.PS[:, j:j + w], d["ones_row_b"][:, :],
                               d["MROW"][0:1, e * T + sub * SUB + j: e * T + sub * SUB + j + w],
                               start=True, stop=True)
                ctx.mm.inc(eng.drain())
            ctx.phase(tensor=tb)
            bc_done = ctx.mm.n
            vv0 = ctx.vv.n
            def vb(eng, e=e, sub=sub, bc_done=bc_done):
                eng.wait_ge(ctx.mm.h, bc_done)
                i1 = eng.tensor_copy(out=mbc[:, 0:SUB], in_=ctx.PS[:, 0:SUB])
                ctx.vv.chain(eng, i1)
                last = None
                for c in range(8):
                    last = eng.tensor_mul(out=xne[:, c * SUB:(c + 1) * SUB],
                                          in0=d["XN"][:, c * T + sub * SUB: c * T + sub * SUB + SUB],
                                          in1=mbc[:, 0:SUB])
                ctx.vv.inc(last)
            ctx.phase(vector=vb)
            xne_done = ctx.vv.n

            def wbias(eng, e=e):
                eng.wait_ge(ctx.wl.h, ctx.wl.n)
                ctx.wl.dma(eng.dma_start(out=d["B1R"][0:1, :], in_=d["w1b_d"][e:e + 1, :]))
                ctx.wl.dma(eng.dma_start(out=d["B2R"][0:1, :], in_=d["w2b_d"][e:e + 1, :]))
            ctx.phase(gpsimd=wbias)
            brow_ready = ctx.wl.n

            for hcb in range(NHC // HCB):
                def w1l(eng, e=e, hcb=hcb):
                    eng.wait_ge(ctx.wl.h, ctx.wl.n)
                    if hcb > 0:
                        eng.wait_ge(ctx.ex.h, ctx.ex.n)
                    for ic in range(8):
                        ctx.wl.dma(eng.dma_start(
                            out=w1b[:, ic * 1024:(ic + 1) * 1024],
                            in_=d["w1_d"][e, ts(ic), hcb * 1024:(hcb + 1) * 1024]))
                ctx.phase(gpsimd=w1l)
                w1_ready = ctx.wl.n
                mmh = ctx.mm.n
                exh = ctx.ex.n
                def th(eng, e=e, hcb=hcb, w1_ready=w1_ready, mmh=mmh, exh=exh,
                       xne_done=xne_done, brow_ready=brow_ready):
                    ctx.wl.wait(eng, w1_ready)
                    eng.wait_ge(ctx.vv.h, xne_done)
                    ctx.wl.wait(eng, brow_ready)
                    for hc in range(HCB):
                        bank = 6 + (hc % 2)
                        if hc >= 2:
                            eng.wait_ge(ctx.ex.h, exh + hc - 1)
                        for ic in range(8):
                            eng.matmul(ctx.PS[:, bank * 512: bank * 512 + SUB],
                                       w1b[:, ic * 1024 + hc * 128: ic * 1024 + (hc + 1) * 128],
                                       xne[:, ic * SUB:(ic + 1) * SUB],
                                       start=(ic == 0), stop=False)
                        eng.matmul(ctx.PS[:, bank * 512: bank * 512 + SUB],
                                   d["B1R"][0:1, hcb * 1024 + hc * 128: hcb * 1024 + (hc + 1) * 128],
                                   d["MROW"][0:1, e * T + sub * SUB: e * T + sub * SUB + SUB],
                                   start=False, stop=True)
                        ctx.mm.inc(eng.drain())
                def sh(eng, hcb=hcb, mmh=mmh, exh=exh):
                    for hc in range(HCB):
                        bank = 6 + (hc % 2)
                        eng.wait_ge(ctx.mm.h, mmh + hc + 1)
                        eng.activation(out=Hs[:, (hcb * HCB + hc) * SUB:(hcb * HCB + hc + 1) * SUB],
                                       in_=ctx.PS[:, bank * 512: bank * 512 + SUB],
                                       func=AF.Relu)
                        ctx.ex.inc(eng.drain())
                ctx.phase(tensor=th, scalar=sh)

            h_done = ctx.ex.n
            for hcb in range(NHC // HCB):
                def w2l(eng, e=e, hcb=hcb):
                    eng.wait_ge(ctx.wl.h, ctx.wl.n)
                    if hcb > 0:
                        eng.wait_ge(ctx.mm.h, ctx.mm.n)
                    for hr in range(HCB):
                        ctx.wl.dma(eng.dma_start(
                            out=w2b[:, hr * 1024:(hr + 1) * 1024],
                            in_=d["w2_d"][e, ts(hcb * HCB + hr), :]))
                ctx.phase(gpsimd=w2l)
                w2_ready = ctx.wl.n
                def to(eng, e=e, hcb=hcb, w2_ready=w2_ready, h_done=h_done):
                    ctx.wl.wait(eng, w2_ready)
                    if hcb == 0:
                        eng.wait_ge(ctx.ex.h, h_done)
                    for tt in range(ntt):
                        for hr in range(HCB):
                            hc = hcb * HCB + hr
                            for j in range(2):
                                eng.matmul(ctx.PS[:, tt * 1024 + j * 512: tt * 1024 + (j + 1) * 512],
                                           Hs[:, hc * SUB + tt * 128: hc * SUB + (tt + 1) * 128],
                                           w2b[:, hr * 1024 + j * 512: hr * 1024 + (j + 1) * 512],
                                           start=(hcb == 0 and hr == 0), stop=False,
                                           skip_group_check=True)
                        if hcb == 0 and hr == 0:
                            pass
                    ctx.mm.inc(eng.drain())
                ctx.phase(tensor=to)
            mmf = ctx.mm.n
            def tf(eng, e=e, sub=sub):
                for tt in range(ntt):
                    for j in range(2):
                        eng.matmul(ctx.PS[:, tt * 1024 + j * 512: tt * 1024 + (j + 1) * 512],
                                   d["MROW"][0:1, e * T + sub * SUB + tt * 128: e * T + sub * SUB + (tt + 1) * 128],
                                   d["B2R"][0:1, j * 512:(j + 1) * 512],
                                   start=False, stop=True, skip_group_check=True)
                ctx.mm.inc(eng.drain())
            ctx.phase(tensor=tf)
            f_ready = ctx.mm.n
            def vf(eng, f_ready=f_ready):
                eng.wait_ge(ctx.mm.h, f_ready)
                last = None
                for tt in range(ntt):
                    for j in range(2):
                        last = eng.tensor_add(
                            out=ga[:, tt * 1024 + j * 512: tt * 1024 + (j + 1) * 512],
                            in0=ga[:, tt * 1024 + j * 512: tt * 1024 + (j + 1) * 512],
                            in1=ctx.PS[:, tt * 1024 + j * 512: tt * 1024 + (j + 1) * 512])
                ctx.vv.inc(last)
            ctx.phase(vector=vf)
            ctx.barrier()

        mmz = ctx.mm.n
        exz = ctx.ex.n
        def tz(eng, sub=sub, mmz=mmz, exz=exz):
            n = 0
            for tt in range(ntt):
                for fcd in range(8):
                    if n >= 2:
                        eng.wait_ge(ctx.ex.h, exz + n - 1)
                    eng.transpose(ctx.PS[0:128, (n % 2) * 512:(n % 2) * 512 + 128],
                                  ga[:, tt * 1024 + fcd * 128: tt * 1024 + (fcd + 1) * 128],
                                  d["idnf_sb"][:, :])
                    ctx.mm.inc(eng.drain())
                    n += 1
        def sz(eng, sub=sub, mmz=mmz, exz=exz):
            n = 0
            for tt in range(ntt):
                for fcd in range(8):
                    eng.wait_ge(ctx.mm.h, mmz + n + 1)
                    eng.activation(out=d["GG"][:, fcd * T + sub * SUB + tt * 128: fcd * T + sub * SUB + (tt + 1) * 128],
                                   in_=ctx.PS[0:128, (n % 2) * 512:(n % 2) * 512 + 128],
                                   func=AF.Silu)
                    ctx.ex.inc(eng.drain())
                    n += 1
        ctx.phase(tensor=tz, scalar=sz)
        ctx.barrier()
    return ctx.ex.n


def build_L3():
    nc = bass.Bass("TRN2", target_bir_lowering=False, num_devices=8)
    st = contextlib.ExitStack()
    w_specs = [
        ("wq_t", [D, D], F32), ("wk_t", [D, D], F32), ("wv_t", [D, D], F32),
        ("ipw", [1, 1], F32), ("Pcls", [128, 8], F32), ("invPcls", [128, 8], F32),
        ("gate_w", [D, 8], F32), ("gate_b8", [8, 1], F32),
        ("w1_d", [8, D, 4096], F32), ("w1b_d", [8, 4096], F32),
        ("w2_d", [8, 4096, D], F32), ("w2b_d", [8, D], F32),
        ("idnf", [128, 128], F32),
    ]
    with nc.Block() as block:
        ctx = Ctx(nc, block, st)
        d = build_common(nc, ctx, w_specs)
        B = 16512
        d["XBF32"] = nc.alloc_sbuf_tensor_at("XBF32", [128, 2048], F32, offset=B + 32768)
        d["VBLK3"] = nc.alloc_sbuf_tensor_at("VBLK3", [128, 2 * 8 * 1040], BF16, offset=B + 131584)
        d["MROW"] = nc.alloc_sbuf_tensor_at("MROW", [1, 8 * 1024], BF16, offset=B + 82432)
        d["GACC3"] = nc.alloc_sbuf_tensor_at("GACC3", [128, 4 * 1024], F32, offset=B + 98816)
        d["HS3"] = nc.alloc_sbuf_tensor_at("HS3", [128, 32 * 512], BF16, offset=B + 0)
        d["W1B"] = nc.alloc_sbuf_tensor_at("W1B", [128, 8 * 1024], BF16, offset=B + 131584)
        d["W2B"] = nc.alloc_sbuf_tensor_at("W2B", [128, 8 * 1024], BF16, offset=B + 147968)
        d["B1R"] = nc.alloc_sbuf_tensor_at("B1R", [1, 4096], BF16, offset=B + 180736)
        d["B2R"] = nc.alloc_sbuf_tensor_at("B2R", [1, 1024], BF16, offset=B + 188928)
        d["XNE"] = nc.alloc_sbuf_tensor_at("XNE", [128, 8 * 512], BF16, offset=B + 40960)
        d["MBC"] = nc.alloc_sbuf_tensor_at("MBC", [128, 512], BF16, offset=B + 190976)
        d["gatew_sb"] = nc.alloc_sbuf_tensor_at("gatew_sb", [128, 64], BF16, offset=B + 206560)
        d["gateb_sb"] = nc.alloc_sbuf_tensor_at("gateb_sb", [128, 1], F32, offset=B + 206688)
        d["idnf_sb"] = nc.alloc_sbuf_tensor_at("idnf_sb", [128, 128], F32, offset=B + 206720)
        ctx.d = d
        emit_setup(ctx, d)
        setup_vv = ctx.vv.n
        emit_zero_guards(ctx, d)
        def wload(eng):
            eng.wait_ge(ctx.wl.h, ctx.wl.n)
            for ic in range(8):
                ctx.wl.dma(eng.dma_start(out=d["WW"][:, ic * 3072:ic * 3072 + 1024],
                                         in_=d["wk_t"][ts(ic), :]))
                ctx.wl.dma(eng.dma_start(out=d["WW"][:, ic * 3072 + 1024:ic * 3072 + 2048],
                                         in_=d["wv_t"][ts(ic), :]))
                ctx.wl.dma(eng.dma_start(out=d["WW"][:, ic * 3072 + 2048:ic * 3072 + 3072],
                                         in_=d["wq_t"][ts(ic), :]))
                ctx.wl.dma(eng.dma_start(out=d["gatew_sb"][:, ic * 8:(ic + 1) * 8],
                                         in_=d["gate_w"][ts(ic), :]))
            ctx.wl.dma(eng.dma_start(out=d["gateb_sb"][0:8, 0:1], in_=d["gate_b8"]))
            ctx.wl.dma(eng.dma_start(out=d["idnf_sb"][:, :], in_=d["idnf"]))
            ctx.wl.dma(eng.dma_start(out=d["scl8"][:, 8:16], in_=d["Pcls"]))
            ctx.wl.dma(eng.dma_start(out=d["maskA"][:, 0:8], in_=d["invPcls"]))
        ctx.phase(gpsimd=wload)
        w_ready = ctx.wl.n
        d["_gw_ready"] = w_ready

        def ldip(eng):
            ctx.ld.dma(eng.dma_start(out=d["sigrow"][0:1, 0:1], in_=d["ipw"]))
        ctx.phase(sync=ldip)
        ip_t = ctx.ld.n
        def s0(eng):
            ctx.ld.wait(eng, ip_t)
            eng.activation(out=d["sigrow"][0:1, 1:2], in_=d["sigrow"][0:1, 0:1], func=AF.Sigmoid)
            ctx.ex.inc(eng.drain())
        ctx.phase(scalar=s0)
        sg_t = ctx.ex.n
        def t0(eng):
            eng.wait_ge(ctx.ex.h, sg_t)
            eng.wait_ge(ctx.vv.h, setup_vv)
            eng.matmul(ctx.PS[:, 0:1], d["ones_row_f"][:, :], d["sigrow"][0:1, 1:2],
                       start=True, stop=True)
            ctx.mm.inc(eng.drain())
        ctx.phase(tensor=t0)
        bc_t = ctx.mm.n
        def v0(eng):
            eng.wait_ge(ctx.mm.h, bc_t)
            ctx.wl.wait(eng, w_ready)
            i1 = eng.tensor_copy(out=d["scl8"][:, 7:8], in_=ctx.PS[:, 0:1])
            ctx.vv.chain(eng, i1)
            i2 = eng.scalar_tensor_tensor(out=d["scl8"][:, 0:8], in0=d["maskA"][:, 0:8],
                                          scalar=d["scl8"][:, 7:8], in1=d["scl8"][:, 8:16],
                                          op0=OP.mult, op1=OP.add)
            ctx.vv.chain(eng, i2)
            i3 = eng.tensor_scalar(out=d["scl8"][:, 0:8], in0=d["scl8"][:, 0:8],
                                   scalar1=0.125, scalar2=None, op0=OP.mult)
            ctx.vv.inc(i3)
        ctx.phase(vector=v0)

        kproj = dict(w_sb=d["WW"], oc0=0, w_oc_total=3072, bias_col=None, w_ready=w_ready)
        vproj = dict(w_sb=d["WW"], oc0=1024, w_oc_total=3072, vb_bc=None, w_ready=w_ready)
        emit_phaseA(ctx, d, kproj, vproj)

        for p in range(T_OWN // TB):
            ctx.barrier()
            def ld(eng, p=p):
                ctx.ld.dma(eng.dma_start(
                    out=d["XF"][:, 0:8 * TB].rearrange("p (c t) -> p c t", c=8),
                    in_=d["x_own"].rearrange("(c p) s -> p c s", p=128)[:, :, p * TB:(p + 1) * TB]))
            ctx.phase(sync=ld)
            emit_ln2(ctx, d, d["XF"], TB, (ctx.ld, ctx.ld.n))
            xn_ready = ctx.vv.n
            ctx.barrier()
            emit_proj_fm(ctx, d["XN"], d["WW"], TB, 2048, 8, d["QQ"], TB, 0, 3072,
                         bias_col=None, w_ready=w_ready, xn_ready=xn_ready, bank0=6)
            ctx.barrier()
            emit_attention(ctx, l3_attn_hgs(d, TB), d["kvk"], d["kvv"], d["QQ"],
                           d["SRAW"], d["LR"], d["lbuf"], TB, d["KBLK"], d["VBLK3"], d["PP"])
            ctx.barrier()
            emit_normalize(ctx, d["SRAW"], d["lbuf"], d["SBF"], TB, d["LB"])
            ctx.barrier()
            emit_l3_gating(ctx, d, xn_ready, TB)
            ctx.barrier()
            emit_l3_moe(ctx, d, TB)
            ctx.barrier()
            emit_update_store(ctx, d, p, d["SBF"], d["GG"])
        finish(ctx)
    st.close()
    return nc



BIGN = BIG_NEG


def own_tokens(c):
    r = c % 4
    idx = []
    for t in range(16):
        g = 4 * t + r
        idx += list(range(g * 128, (g + 1) * 128))
    return np.array(idx)


def col8(v):
    return np.ascontiguousarray(v.reshape(-1, 128).T)


def mkF1(G1):
    out = np.empty((4, 512, 512), np.float32)
    for c in range(4):
        A = G1[0][:, c, :]
        out[c] = np.broadcast_to(A.T[:, None, :, None], (32, 16, 32, 16)).reshape(512, 512)
    return out


def mkF2(G2):
    out = np.empty((4, 512, 512), np.float32)
    for c in range(4):
        Bm = G2[c][:, 0, :]
        out[c] = np.broadcast_to(Bm.T[None, :, None, :], (32, 16, 32, 16)).reshape(512, 512)
    return out


def rope_tables():
    HD = 64
    pos = np.arange(S, dtype=np.float32)
    inv = 1.0 / 10000.0 ** (np.arange(0, HD, 2, dtype=np.float32) / HD)
    ang = np.repeat(np.outer(pos, inv), 2, axis=-1)
    cos_fm = np.tile(np.cos(ang).T, (2, 1)).astype(np.float32)
    sin_fm = np.tile(np.sin(ang).T, (2, 1)).astype(np.float32)
    return cos_fm, sin_fm


def rt2_mat():
    R = np.zeros((64, 64), np.float32)
    for i in range(32):
        R[2 * i, 2 * i + 1] = -1.0
        R[2 * i + 1, 2 * i] = 1.0
    return np.tile(R.T, (2, 1)).astype(np.float32)


def maskA_for(r):
    tri = np.where(np.arange(128)[:, None] <= np.arange(128)[None, :], 0.0, BIGN).astype(np.float32)
    out = np.zeros((4, 128, 128), np.float32)
    for b in range(4):
        if b == r:
            out[b] = tri
        elif b > r:
            out[b] = BIGN
    return np.ascontiguousarray(np.concatenate([out[b] for b in range(4)], axis=1))


def maskW_for(r):
    kr = np.arange(128)[:, None]
    qc = np.arange(128)[None, :]
    triL = np.where(kr >= qc, 0.0, BIGN).astype(np.float32)
    triR = np.where(kr <= qc, 0.0, BIGN).astype(np.float32)
    FULL = np.full((128, 128), BIGN, np.float32)
    ZERO = np.zeros((128, 128), np.float32)

    def slotmask(s):
        dlt = s - 1 - r
        if dlt == -1:
            return triL
        if dlt == 0:
            return ZERO
        if dlt == 1:
            return triR
        return FULL

    norm = np.concatenate([slotmask(s) for s in range(7)], axis=1)
    left = norm.copy()
    if r == 0:
        left[:, 0:128] = FULL
    right = norm.copy()
    if r == 3:
        right[:, 5 * 128:6 * 128] = FULL
    return np.ascontiguousarray(np.concatenate([norm, left, right], axis=1))


def pcls_mats():
    rr = np.arange(128)
    cls = (rr // 8) % 8
    P = (cls[:, None] == np.arange(8)[None, :]).astype(np.float32)
    return P, 1.0 - P


def in_maps_for_layer(layer, x, I):
    C = np.ascontiguousarray
    maps = []
    xT = [C(x[0].T), C(x[1].T)]
    if layer == 0:
        cos_fm, sin_fm = rope_tables()
        rt2 = rt2_mat()
        F = dict(F1R=mkF1(I["L0_ffn_G1_R"]), F2R=mkF2(I["L0_ffn_G2_R"]),
                 F1I=mkF1(I["L0_ffn_G1_I"]), F2I=mkF2(I["L0_ffn_G2_I"]))
        bffn = np.concatenate([I["L0_ffn_bias_real"], I["L0_ffn_bias_imag"]])
    if layer == 3:
        P, invP = pcls_mats()
    for c in range(8):
        b, r = c // 4, c % 4
        ot = own_tokens(c)
        m = {"x_full": xT[b], "x_own": C(x[b][ot].T)}
        if layer == 0:
            m.update({
                "wq": I["L0_attn_Wq"], "wk": I["L0_attn_Wk"],
                "wv": I["L0_attn_Wv"], "wo": I["L0_attn_Wo"],
                "bq_col": col8(I["L0_attn_bq"]), "bk_col": col8(I["L0_attn_bk"]),
                "bo_col": col8(I["L0_attn_bo"]),
                "bv_row": I["L0_attn_bv"].reshape(1, 1024),
                "bffn_col": col8(bffn),
                "F1R": F["F1R"], "F2R": F["F2R"], "F1I": F["F1I"], "F2I": F["F2I"],
                "cos_all": cos_fm, "sin_all": sin_fm,
                "cos_own": C(cos_fm[:, ot]), "sin_own": C(sin_fm[:, ot]),
                "rt2": rt2, "maskA_in": maskA_for(r),
            })
        elif layer == 1:
            m.update({
                "wqkv": C(I["L1_attn_qkv_w"].T),
                "bqkv_col": col8(I["L1_attn_qkv_b"]),
                "bv_row": I["L1_attn_qkv_b"][2048:].reshape(1, 1024),
                "wout": C(I["L1_attn_out_w"].T),
                "bout_col": col8(I["L1_attn_out_b"]),
                "wcb": C(I["L1_ffn_CB"].T),
                "scb_col": col8(I["L1_ffn_SCB"]),
                "bffn_col": col8(I["L1_ffn_bias"]),
            })
        elif layer == 2:
            m.update({
                "wqkv": I["L2_attn_Wqkv"],
                "bqkv_col": col8(I["L2_attn_bqkv"]),
                "bv_row": I["L2_attn_bqkv"][2048:].reshape(1, 1024),
                "wout": I["L2_attn_Wout"],
                "bout_col": col8(I["L2_attn_bout"]),
                "wg_t": C(I["L2_ffn_w_gate"].T), "wu_t": C(I["L2_ffn_w_up"].T),
                "wd_t": C(I["L2_ffn_w_down"].T),
                "maskW_in": maskW_for(r),
            })
        else:
            m.update({
                "wq_t": C(I["L3_attn_Wq"].T), "wk_t": C(I["L3_attn_Wk"].T),
                "wv_t": C(I["L3_attn_Wv"].T),
                "ipw": np.asarray(I["L3_attn_inter_palace_weight"], np.float32).reshape(1, 1),
                "Pcls": P, "invPcls": invP,
                "gate_w": I["L3_ffn_gate_w"],
                "gate_b8": I["L3_ffn_gate_b"].reshape(8, 1),
                "w1_d": I["L3_ffn_w1"], "w1b_d": I["L3_ffn_b1"],
                "w2_d": I["L3_ffn_w2"], "w2b_d": I["L3_ffn_b2"],
                "idnf": np.eye(128, dtype=np.float32),
            })
        maps.append(m)
    return maps


def reassemble(x_prev, results):
    out = np.array(x_prev)
    for c in range(8):
        b = c // 4
        ot = own_tokens(c)
        out[b][ot] = results[c]["x_next"].T
    return out



_BUILD_CACHE = {}


def _get_nc(layer):
    if layer not in _BUILD_CACHE:
        set_sizes(s=8192, ta=1024, tb=(512 if layer == 2 else 1024), t_own=2048, kcb=8)
        _BUILD_CACHE[layer] = {0: build_L0, 1: build_L1, 2: build_L2, 3: build_L3}[layer]()
    return _BUILD_CACHE[layer]


def kernel(**inputs):
    x = np.asarray(inputs["L0_attn_x"], np.float32)
    for layer in range(4):
        nc = _get_nc(layer)
        maps = in_maps_for_layer(layer, x, inputs)
        res = run_bass_kernel_spmd(nc, maps, core_ids=list(range(8)))
        x = reassemble(x, res.results)
    return x
